# version 05
# speedup vs baseline: 1.7834x; 1.0312x over previous
from contextlib import ExitStack

import numpy as np
import concourse.bass as bass
import concourse.mybir as mybir
from concourse.bass_utils import run_bass_kernel_spmd

F32 = mybir.dt.float32
BF16 = mybir.dt.bfloat16
AF = mybir.ActivationFunctionType
ALU = mybir.AluOpType
AX = mybir.AxisListType

MAGIC = 12582912.0
NCORE = 8
T = 16384
D = 1024
RPC = T // NCORE
NT = RPC // 128
NQ = 640
NC0 = 516
Q0 = T - NQ
NEG = -30000.0


class Prog:
    def __init__(self, nc, st):
        self.nc = nc
        self.st = st
        self.block = st.enter_context(nc.Block())
        self.sems = {}

    def sem(self, name):
        if name not in self.sems:
            self.sems[name] = [self.st.enter_context(self.nc.semaphore(name)), 0]
        return self.sems[name]

    def inc(self, name, instr, k=1):
        s = self.sem(name)
        instr.then_inc(s[0], k)
        s[1] += k

    def n(self, name):
        return self.sem(name)[1]

    def wait(self, eng, name, val):
        eng.wait_ge(self.sem(name)[0], val)



class DeferredBlock:

    _ENGINES = ("sync", "gpsimd", "tensor", "scalar", "vector")

    def __init__(self, rb):
        self.rb = rb
        self.q = {e: [] for e in self._ENGINES}

    def __getattr__(self, name):
        if name in self._ENGINES:
            return lambda f: self.q[name].append(f)
        raise AttributeError(name)

    def flush(self):
        for e in self._ENGINES:
            fs = self.q[e]
            if not fs:
                continue
            def body(eng, fs=fs):
                for f in fs:
                    f(eng)
            getattr(self.rb, e)(body)


def _stage_w(w):
    return np.ascontiguousarray(
        np.asarray(w, dtype=np.float32).reshape(8, 128, 1024)
        .transpose(1, 0, 2).reshape(128, 8192))



def build_la(upto=6, dbg=False):
    nc = bass.Bass("TRN2", target_bir_lowering=False)
    x0s = nc.dram_tensor("x0s", [RPC, D], F32, kind="ExternalInput").ap()
    xq = nc.dram_tensor("xq", [NQ, D], F32, kind="ExternalInput").ap()
    xctx = nc.dram_tensor("xctx", [NC0, D], F32, kind="ExternalInput").ap()
    Wq = nc.dram_tensor("Wq", [128, 8192], F32, kind="ExternalInput").ap()
    Wk = nc.dram_tensor("Wk", [128, 8192], F32, kind="ExternalInput").ap()
    Wv = nc.dram_tensor("Wv", [128, 8192], F32, kind="ExternalInput").ap()
    Wo = nc.dram_tensor("Wo", [128, 8192], F32, kind="ExternalInput").ap()
    maskadd = nc.dram_tensor("maskadd", [NQ, NC0], F32, kind="ExternalInput").ap()
    validc = nc.dram_tensor("validc", [128, 5], F32, kind="ExternalInput").ap()
    cmcol = nc.dram_tensor("cmcol", [128, 1], F32, kind="ExternalInput").ap()
    identf = nc.dram_tensor("identf", [128, 128], F32, kind="ExternalInput").ap()
    x1s = nc.dram_tensor("x1s", [RPC, D], F32, kind="ExternalOutput").ap()
    mx = nc.dram_tensor("mx", [1, 1], F32, kind="ExternalOutput").ap()
    if dbg:
        dpj = nc.dram_tensor("dpj", [NQ, D], F32, kind="ExternalOutput").ap()
        dw = nc.dram_tensor("dw", [128, 16 * NC0], F32, kind="ExternalOutput").ap()


    with ExitStack() as st:
        E = st.enter_context
        sXQC = E(nc.sbuf_tensor("sXQC", [128, 5 * D], F32))
        sXqT = E(nc.sbuf_tensor("sXqT", [128, 8 * NQ], BF16))
        sXcT = E(nc.sbuf_tensor("sXcT", [128, 8 * NC0], BF16))
        sWb = [E(nc.sbuf_tensor(f"sWb{i}", [128, 8192], BF16)) for i in range(2)]
        sQT = E(nc.sbuf_tensor("sQT", [128, 8 * NQ], BF16))
        sKT = E(nc.sbuf_tensor("sKT", [128, 8 * NC0], BF16))
        sV = E(nc.sbuf_tensor("sV", [128, 5 * D], BF16))
        sLG = E(nc.sbuf_tensor("sLG", [128, 16 * NC0], F32))
        sMsk = E(nc.sbuf_tensor("sMsk", [128, 2 * NC0], F32))
        sVld = E(nc.sbuf_tensor("sVld", [128, 5], F32))
        sCm = E(nc.sbuf_tensor("sCm", [128, 1], F32))
        sId = E(nc.sbuf_tensor("sId", [128, 128], F32))
        sNM = E(nc.sbuf_tensor("sNM", [128, 16], F32))
        sES = E(nc.sbuf_tensor("sES", [128, 16], F32))
        sER = E(nc.sbuf_tensor("sER", [128, 16], F32))
        sWT = E(nc.sbuf_tensor("sWT", [128, 5 * 128], BF16))
        sAT = E(nc.sbuf_tensor("sAT", [128, 8 * NQ], BF16))
        sPJ = E(nc.sbuf_tensor("sPJ", [128, 5 * D], F32))
        sX0 = E(nc.sbuf_tensor("sX0", [128, 2 * D], F32))
        sSq = E(nc.sbuf_tensor("sSq", [128, D], F32))
        sX1 = E(nc.sbuf_tensor("sX1", [128, 2 * D], F32))
        sMB = E(nc.sbuf_tensor("sMB", [128, 16], F32))
        sSm = E(nc.sbuf_tensor("sSm", [128, 8], F32))
        sR1 = E(nc.sbuf_tensor("sR1", [1, 128], F32))
        sEps = E(nc.sbuf_tensor("sEps", [128, 1], F32))
        PP = [E(nc.psum_tensor(f"PP{i}", [128, 512], F32)) for i in range(8)]

        p = Prog(nc, st)
        B = DeferredBlock(p.block)

        @B.sync
        def _(sy):
            p.inc("dA", sy.dma_start(out=sXQC[:, :].rearrange("p (t d) -> p t d", d=D),
                  in_=xq.rearrange("(t p) d -> p t d", p=128)), 16)
            p.inc("dA", sy.dma_start(out=sVld[:, :], in_=validc[:, :]), 16)
            p.inc("dA", sy.dma_start(out=sCm[:, :], in_=cmcol[:, :]), 16)
            p.inc("dA", sy.dma_start(out=sId[:, :], in_=identf[:, :]), 16)

        @B.gpsimd
        def _(g):
            p.inc("dW0", g.dma_start(out=sWb[0][:, :], in_=Wq[:, :]), 16)
            p.inc("dW1", g.dma_start(out=sWb[1][:, :], in_=Wk[:, :]), 16)

        @B.vector
        def _(v):
            p.inc("VV", v.memset(sEps[:, :], 1e-5))

        @B.sync
        def _(sy):
            p.wait(sy, "Ttr", 40)
            p.inc("dA", sy.dma_start(out=sXQC[:, 0:4 * D].rearrange("p (t d) -> p t d", d=D),
                  in_=xctx[0:512, :].rearrange("(t p) d -> p t d", p=128)), 16)
            p.inc("dA", sy.dma_start(out=sXQC[0:4, 4 * D:5 * D], in_=xctx[512:516, :]), 16)

        @B.tensor
        def _(t):
            p.wait(t, "dA", 16 * 4)
            for i in range(80):
                if i == 40:
                    p.wait(t, "dA", 16 * 6)
                if i >= 8:
                    p.wait(t, "Str", i - 7)
                if i < 40:
                    t_, c_ = divmod(i, 8)
                    src = sXQC[:, t_ * D + 128 * c_: t_ * D + 128 * (c_ + 1)]
                    out = PP[i % 8][:, 0:128]
                elif i < 72:
                    t_, c_ = divmod(i - 40, 8)
                    src = sXQC[:, t_ * D + 128 * c_: t_ * D + 128 * (c_ + 1)]
                    out = PP[i % 8][:, 0:128]
                else:
                    c_ = i - 72
                    src = sXQC[0:4, 4 * D + 128 * c_: 4 * D + 128 * (c_ + 1)]
                    out = PP[i % 8][0:128, 0:4]
                ident = sId[:, :] if i < 72 else sId[0:4, 0:4]
                p.inc("Ttr", t.transpose(out, src, ident))

        @B.scalar
        def _(s):
            for i in range(80):
                p.wait(s, "Ttr", i + 1)
                if i < 40:
                    t_, c_ = divmod(i, 8)
                    dst = sXqT[:, c_ * NQ + 128 * t_: c_ * NQ + 128 * (t_ + 1)]
                    ps = PP[i % 8][:, 0:128]
                elif i < 72:
                    t_, c_ = divmod(i - 40, 8)
                    dst = sXcT[:, c_ * NC0 + 128 * t_: c_ * NC0 + 128 * (t_ + 1)]
                    ps = PP[i % 8][:, 0:128]
                else:
                    c_ = i - 72
                    dst = sXcT[:, c_ * NC0 + 512: c_ * NC0 + 516]
                    ps = PP[i % 8][:, 0:4]
                p.inc("Str", s.copy(dst, ps))

        if upto < 2:
            @B.sync
            def _(sy):
                p.wait(sy, "Str", 80)
            B.flush()
            return nc
        def _p2_groups():
            out = []
            for oc in range(8):
                for hf in range(2):
                    out.append((0, oc, hf))
            for oc in range(8):
                for hf in range(2):
                    out.append((1, oc, hf))
            for ct in range(5):
                for hf in range(2):
                    out.append((2, ct, hf))
            return out

        @B.tensor
        def _(t):
            p.wait(t, "Str", 80)
            for g, (ph, a, hf) in enumerate(_p2_groups()):
                if ph == 0 and a == 0 and hf == 0:
                    p.wait(t, "dW0", 16)
                if ph == 1 and a == 0 and hf == 0:
                    p.wait(t, "dW1", 16)
                if ph == 2 and a == 0 and hf == 0:
                    p.wait(t, "dW0", 32)
                if g >= 8:
                    p.wait(t, "Sqk", g - 7)
                pb = PP[g % 8]
                wsb = sWb[0] if ph in (0, 2) else sWb[1]
                last = None
                for kc in range(8):
                    if ph == 0:
                        lhsT = wsb[:, kc * 1024 + 128 * a: kc * 1024 + 128 * (a + 1)]
                        rhs = sXqT[:, kc * NQ + 320 * hf: kc * NQ + 320 * (hf + 1)]
                        out = pb[:, 0:320]
                    elif ph == 1:
                        lhsT = wsb[:, kc * 1024 + 128 * a: kc * 1024 + 128 * (a + 1)]
                        rhs = sXcT[:, kc * NC0 + 258 * hf: kc * NC0 + 258 * (hf + 1)]
                        out = pb[:, 0:258]
                    else:
                        rows = 128 if a < 4 else 4
                        lhsT = sXcT[:, kc * NC0 + 128 * a: kc * NC0 + 128 * a + rows]
                        rhs = wsb[:, kc * 1024 + 512 * hf: kc * 1024 + 512 * (hf + 1)]
                        out = pb[0:rows, 0:512]
                    last = t.matmul(out, lhsT, rhs, start=(kc == 0), stop=(kc == 7))
                p.inc("Tqk", last)

        @B.gpsimd
        def _(g):
            p.wait(g, "Tqk", 16)
            p.inc("dW0", g.dma_start(out=sWb[0][:, :], in_=Wv[:, :]), 16)
            p.wait(g, "Tqk", 32)
            p.inc("dW1", g.dma_start(out=sWb[1][:, :], in_=Wo[:, :]), 16)

        @B.scalar
        def _(s):
            for g, (ph, a, hf) in enumerate(_p2_groups()):
                p.wait(s, "Tqk", g + 1)
                pb = PP[g % 8]
                if ph == 0:
                    dst = sQT[:, a * NQ + 320 * hf: a * NQ + 320 * (hf + 1)]
                    src = pb[:, 0:320]
                elif ph == 1:
                    dst = sKT[:, a * NC0 + 258 * hf: a * NC0 + 258 * (hf + 1)]
                    src = pb[:, 0:258]
                else:
                    rows = 128 if a < 4 else 4
                    dst = sV[0:rows, a * D + 512 * hf: a * D + 512 * (hf + 1)]
                    src = pb[0:rows, 0:512]
                p.inc("Sqk", s.copy(dst, src))

        if upto < 3:
            @B.sync
            def _(sy):
                p.wait(sy, "Sqk", 42)
            B.flush()
            return nc
        @B.sync
        def _(sy):
            for qt in range(5):
                if qt >= 2:
                    p.wait(sy, "Vlg", 8 * (qt - 1))
                p.inc(f"dM{qt % 2}", sy.dma_start(
                    out=sMsk[:, (qt % 2) * NC0:(qt % 2 + 1) * NC0],
                    in_=maskadd[128 * qt:128 * (qt + 1), :]), 16)

        for qt in range(5):
            @B.tensor
            def _(t, qt=qt):
                if qt == 0:
                    p.wait(t, "Sqk", 42)
                else:
                    p.wait(t, "Sav", 16 * qt)
                    p.wait(t, "Swt", 80 * qt)
                for pr in range(8):
                    gidx = 8 * qt + pr
                    if gidx >= 2:
                        p.wait(t, "Vlg", gidx - 1)
                    for sub in range(4):
                        h = 2 * pr + sub // 2
                        ch = sub % 2
                        pb = PP[4 * (pr % 2) + sub]
                        nn = min(258 * (ch + 1), NC0) - 258 * ch
                        lhsT = sQT[64 * (h % 2):64 * (h % 2) + 64,
                                   (h // 2) * NQ + 128 * qt:(h // 2) * NQ + 128 * (qt + 1)]
                        rhs = sKT[64 * (h % 2):64 * (h % 2) + 64,
                                  (h // 2) * NC0 + 258 * ch:(h // 2) * NC0 + 258 * ch + nn]
                        last = t.matmul(pb[:, 0:nn], lhsT, rhs, start=True, stop=True)
                        if sub == 3:
                            p.inc("Tlg", last)

            @B.vector
            def _(v, qt=qt):
                p.wait(v, f"dM{qt % 2}", 16 * (qt // 2 + 1))
                for pr in range(8):
                    p.wait(v, "Tlg", 8 * qt + pr + 1)
                    last = None
                    for sub in range(4):
                        h = 2 * pr + sub // 2
                        ch = sub % 2
                        pb = PP[4 * (pr % 2) + sub]
                        nn = min(258 * (ch + 1), NC0) - 258 * ch
                        last = v.scalar_tensor_tensor(
                            sLG[:, h * NC0 + 258 * ch: h * NC0 + 258 * ch + nn],
                            pb[:, 0:nn], 0.125,
                            sMsk[:, (qt % 2) * NC0 + 258 * ch:(qt % 2) * NC0 + 258 * ch + nn],
                            ALU.mult, ALU.add)
                    p.inc("Vlg", last)
                p.wait(v, "Vlg", 8 * (qt + 1))
                p.inc("Vsm", v.tensor_reduce(
                    sNM[:, 0:16], sLG[:, :].rearrange("p (h c) -> p h c", c=NC0),
                    axis=AX.X, op=ALU.max, negate=True))

            @B.scalar
            def _(s, qt=qt):
                p.wait(s, "Vsm", qt + 1)
                for h in range(16):
                    p.inc("Ssm", s.activation(
                        sLG[:, h * NC0:(h + 1) * NC0], sLG[:, h * NC0:(h + 1) * NC0],
                        AF.Exp, bias=sNM[:, h:h + 1]))

            @B.vector
            def _(v, qt=qt):
                p.wait(v, "Ssm", 16 * (qt + 1))
                p.inc("VS", v.tensor_reduce(
                    sES[:, 0:16], sLG[:, :].rearrange("p (h c) -> p h c", c=NC0),
                    axis=AX.X, op=ALU.add))
                p.wait(v, "VS", p.n("VS"))
                p.inc("VS", v.reciprocal(sER[:, 0:16], sES[:, 0:16]))
                p.wait(v, "VS", p.n("VS"))
                p.inc("VS", v.tensor_scalar_mul(sER[:, 0:16], sER[:, 0:16],
                                                sVld[:, qt:qt + 1]))
                p.wait(v, "VS", p.n("VS"))
                for h in range(16):
                    p.inc("Vw", v.tensor_scalar_mul(
                        sLG[:, h * NC0:(h + 1) * NC0],
                        sLG[:, h * NC0:(h + 1) * NC0], sER[:, h:h + 1]))

            for h in range(16):
                @B.tensor
                def _(t, qt=qt, h=h):
                    p.wait(t, "Vw", 16 * qt + h + 1)
                    for cc in range(5):
                        nn = 128 if cc < 4 else 4
                        idx = (qt * 16 + h) * 5 + cc
                        if idx >= 4:
                            p.wait(t, "Swt", idx - 3)
                        src = sLG[:, h * NC0 + 128 * cc: h * NC0 + 128 * cc + nn]
                        p.inc("Twt", t.transpose(PP[3 + idx % 4][0:nn, 0:128], src,
                                                 sId[:, :]))

                @B.scalar
                def _(s, qt=qt, h=h):
                    for cc in range(5):
                        nn = 128 if cc < 4 else 4
                        idx = (qt * 16 + h) * 5 + cc
                        p.wait(s, "Twt", idx + 1)
                        p.inc("Swt", s.copy(sWT[0:nn, 128 * cc:128 * (cc + 1)],
                                            PP[3 + idx % 4][0:nn, 0:128]))

                @B.tensor
                def _(t, qt=qt, h=h):
                    p.wait(t, "Swt", (qt * 16 + h) * 5 + 5)
                    if qt * 16 + h > 0:
                        p.wait(t, "Sav", qt * 16 + h)
                    last = None
                    for cc in range(5):
                        nn = 128 if cc < 4 else 4
                        lhsT = sV[0:nn, cc * D + 64 * h: cc * D + 64 * (h + 1)]
                        rhs = sWT[0:nn, 128 * cc:128 * (cc + 1)]
                        last = t.matmul(PP[7][0:64, 0:128], lhsT, rhs,
                                        start=(cc == 0), stop=(cc == 4))
                    p.inc("Tav", last)

                @B.scalar
                def _(s, qt=qt, h=h):
                    p.wait(s, "Tav", qt * 16 + h + 1)
                    p.inc("Sav", s.copy(
                        sAT[64 * (h % 2):64 * (h % 2) + 64,
                            (h // 2) * NQ + 128 * qt:(h // 2) * NQ + 128 * (qt + 1)],
                        PP[7][0:64, 0:128]))

        if upto < 4:
            @B.sync
            def _(sy):
                p.wait(sy, "Sav", 80)
            B.flush()
            return nc
        @B.tensor
        def _(t):
            p.wait(t, "Sav", 80)
            p.wait(t, "dW1", 32)
            for g in range(10):
                tt, hf = divmod(g, 2)
                if g >= 8:
                    p.wait(t, "Spj", g - 7)
                pb = PP[g % 8]
                last = None
                for kc in range(8):
                    lhsT = sAT[:, kc * NQ + 128 * tt: kc * NQ + 128 * (tt + 1)]
                    rhs = sWb[1][:, kc * 1024 + 512 * hf: kc * 1024 + 512 * (hf + 1)]
                    last = t.matmul(pb[:, 0:512], lhsT, rhs,
                                    start=(kc == 0), stop=(kc == 7))
                p.inc("Tpj", last)

        @B.scalar
        def _(s):
            for g in range(10):
                tt, hf = divmod(g, 2)
                p.wait(s, "Tpj", g + 1)
                p.inc("Spj", s.copy(sPJ[:, tt * D + 512 * hf: tt * D + 512 * (hf + 1)],
                                    PP[g % 8][:, 0:512]))

        if upto < 5:
            @B.sync
            def _(sy):
                p.wait(sy, "Spj", 10)
            B.flush()
            return nc
        if dbg:
            @B.sync
            def _(sy):
                p.wait(sy, "Spj", 10)
                for tt in range(5):
                    p.inc("dD", sy.dma_start(out=dpj[128 * tt:128 * (tt + 1), :],
                          in_=sPJ[:, tt * D:(tt + 1) * D]), 16)
                p.inc("dD", sy.dma_start(out=dw[:, :], in_=sLG[:, :]), 16)

                p.wait(sy, "dD", p.n("dD"))

        vS = []
        tot = 0
        for j in range(NT):
            tot += (1 if j >= 11 else 0) + 7
            vS.append(tot)

        def pos_y(j):
            return (vS[j - 1] if j > 0 else 0) + 1

        def pos_sqsum(j):
            return (vS[j - 1] if j > 0 else 0) + (1 if j >= 11 else 0) + 2

        def pos_var2(j):
            return vS[j] - 1

        @B.sync
        def _(sy):
            for j in range(NT):
                if j >= 2:
                    p.wait(sy, "Vx1", j - 1)
                    p.inc(f"dO{j % 2}", sy.dma_start(
                        out=x1s[128 * (j - 2):128 * (j - 1), :],
                        in_=sX1[:, ((j - 2) % 2) * D:((j - 2) % 2 + 1) * D]), 16)
                p.inc(f"dX{j % 2}", sy.dma_start(out=sX0[:, (j % 2) * D:(j % 2 + 1) * D],
                      in_=x0s[128 * j:128 * (j + 1), :]), 16)
            for j in (NT - 2, NT - 1):
                p.wait(sy, "Vx1", j + 1)
                p.inc(f"dO{j % 2}", sy.dma_start(out=x1s[128 * j:128 * (j + 1), :],
                      in_=sX1[:, (j % 2) * D:(j % 2 + 1) * D]), 16)

        Vb5 = 1

        @B.vector
        def _(v):
            for j in range(NT):
                pa = j % 2
                p.wait(v, f"dX{j % 2}", 16 * (j // 2 + 1))
                y = sX0[:, pa * D:(pa + 1) * D]
                if j >= 11:
                    p.wait(v, "Spj", 2 * (j - 11) + 2)
                    p.inc("VV", v.scalar_tensor_tensor(
                        y, sPJ[:, (j - 11) * D:(j - 10) * D], sCm[:, 0:1], y,
                        ALU.mult, ALU.add))
                    p.wait(v, "VV", p.n("VV"))
                p.inc("VV", v.reduce_sum(sSm[:, pa:pa + 1], y, axis=AX.X))
                p.wait(v, "SC", 3 * j + 1)
                p.inc("VV", v.reduce_sum(sSm[:, 2 + pa:3 + pa], sSq[:, :], axis=AX.X))
                p.wait(v, "VV", p.n("VV"))
                p.inc("VV", v.tensor_scalar_mul(sSm[:, pa:pa + 1], sSm[:, pa:pa + 1],
                                                1.0 / D))
                p.wait(v, "VV", p.n("VV"))
                p.inc("VV", v.tensor_mul(sSm[:, 4 + pa:5 + pa], sSm[:, pa:pa + 1],
                                         sSm[:, pa:pa + 1]))
                p.wait(v, "VV", p.n("VV"))
                p.inc("VV", v.tensor_scalar(sSm[:, 2 + pa:3 + pa],
                                            sSm[:, 2 + pa:3 + pa], 1.0 / D, None,
                                            ALU.mult))
                p.wait(v, "VV", p.n("VV"))
                p.inc("VV", v.tensor_sub(sSm[:, 2 + pa:3 + pa],
                                         sSm[:, 2 + pa:3 + pa],
                                         sSm[:, 4 + pa:5 + pa]))
                p.wait(v, "SC", 3 * j + 3)
                if j >= 2:
                    p.wait(v, f"dO{j % 2}", 16 * (j // 2))
                p.inc("VV", v.tensor_scalar(sX1[:, pa * D:(pa + 1) * D], y,
                                            sSm[:, pa:pa + 1], sSm[:, 6 + pa:7 + pa],
                                            ALU.subtract, ALU.mult))
                p.wait(v, "VV", p.n("VV"))
                p.inc("Vx1", v.reduce_max(sMB[:, j:j + 1],
                                          sX1[:, pa * D:(pa + 1) * D], axis=AX.X,
                                          apply_absolute_value=True))

        @B.scalar
        def _(s):
            for j in range(NT):
                pa = j % 2
                y = sX0[:, pa * D:(pa + 1) * D]
                p.wait(s, f"dX{j % 2}", 16 * (j // 2 + 1))
                if j >= 11:
                    p.wait(s, "VV", Vb5 + pos_y(j))
                if j >= 1:
                    p.wait(s, "VV", Vb5 + pos_sqsum(j - 1))
                p.inc("SC", s.activation(sSq[:, :], y, AF.Square))
                p.wait(s, "VV", Vb5 + pos_var2(j))
                p.inc("SC", s.activation(sSm[:, 4 + pa:5 + pa],
                                         sSm[:, 2 + pa:3 + pa], AF.Ln,
                                         bias=sEps[:, 0:1]))
                p.wait(s, "SC", p.n("SC"))
                p.inc("SC", s.activation(sSm[:, 6 + pa:7 + pa],
                                         sSm[:, 4 + pa:5 + pa], AF.Exp, scale=-0.5))

        if upto < 6:
            @B.sync
            def _(sy):
                p.wait(sy, "dO0", 14 * 16)
                p.wait(sy, "dO1", 14 * 16)
            return nc
        @B.vector
        def _(v):
            p.wait(v, "Vx1", NT)
            p.inc("Vmx", v.reduce_max(sSm[:, 7:8], sMB[:, 0:16], axis=AX.X))

        @B.tensor
        def _(t):
            p.wait(t, "Vmx", 1)
            p.inc("Tmx", t.transpose(PP[0][0:1, 0:128], sSm[:, 7:8], sId[:, :]))

        @B.scalar
        def _(s):
            p.wait(s, "Tmx", 1)
            p.inc("SC", s.copy(sR1[0:1, :], PP[0][0:1, 0:128]))

        @B.vector
        def _(v):
            p.wait(v, "SC", 49)
            p.inc("Vmx", v.reduce_max(sSm[0:1, 0:1], sR1[0:1, :], axis=AX.X))

        @B.sync
        def _(sy):
            p.wait(sy, "Vmx", 2)
            p.inc("dOm", sy.dma_start(out=mx[:, :], in_=sSm[0:1, 0:1]), 16)
            p.wait(sy, "dO0", 8 * 16)
            p.wait(sy, "dO1", 8 * 16)
            p.wait(sy, "dOm", 16)

        B.flush()
    return nc



def build_lb():
    nc = bass.Bass("TRN2", target_bir_lowering=False)
    x1i = nc.dram_tensor("x1i", [RPC, D], F32, kind="ExternalInput").ap()
    mxs = nc.dram_tensor("mxs", [1, 8], F32, kind="ExternalInput").ap()
    WtT = nc.dram_tensor("WtT", [128, 8192], F32, kind="ExternalInput").ap()
    biasr = nc.dram_tensor("biasr", [1, D], F32, kind="ExternalInput").ap()
    identf = nc.dram_tensor("identf", [128, 128], F32, kind="ExternalInput").ap()
    x2s = nc.dram_tensor("x2s", [RPC, D], F32, kind="ExternalOutput").ap()
    cs = nc.dram_tensor("cs", [1, D], F32, kind="ExternalOutput").ap()

    with ExitStack() as st:
        E = st.enter_context
        sX1 = E(nc.sbuf_tensor("sX1", [128, NT * D], F32))
        sWt = E(nc.sbuf_tensor("sWt", [128, 8192], F32))
        sQW = E(nc.sbuf_tensor("sQW", [128, 8192], BF16))
        sQW2 = E(nc.sbuf_tensor("sQW2", [128, 8192], BF16))
        sQ3 = E(nc.sbuf_tensor("sQ3", [128, 2 * D], F32))
        sQT = E(nc.sbuf_tensor("sQT", [128, 2 * D], BF16))
        sX2 = E(nc.sbuf_tensor("sX2", [128, 2 * D], F32))
        sSq = E(nc.sbuf_tensor("sSq", [128, D], F32))
        sBt = E(nc.sbuf_tensor("sBt", [128, D], F32))
        sMx = E(nc.sbuf_tensor("sMx", [1, 8], F32))
        sBr = E(nc.sbuf_tensor("sBr", [1, D], F32))
        sId = E(nc.sbuf_tensor("sId", [128, 128], F32))
        sOnR = E(nc.sbuf_tensor("sOnR", [1, 128], F32))
        sOnC = E(nc.sbuf_tensor("sOnC", [128, 1], F32))
        sAs = E(nc.sbuf_tensor("sAs", [128, 1], F32))
        sR1 = E(nc.sbuf_tensor("sR1", [1, 128], F32))
        sS1 = E(nc.sbuf_tensor("sS1", [1, 8], F32))
        sCl = E(nc.sbuf_tensor("sCl", [128, 4], F32))
        sSm = E(nc.sbuf_tensor("sSm", [128, 8], F32))
        sCS = E(nc.sbuf_tensor("sCS", [1, D], F32))
        sEps = E(nc.sbuf_tensor("sEps", [128, 1], F32))
        PP = [E(nc.psum_tensor(f"PP{i}", [128, 512], F32)) for i in range(8)]

        p = Prog(nc, st)
        B = DeferredBlock(p.block)

        @B.sync
        def _(sy):
            p.inc("dA", sy.dma_start(out=sX1[:, :].rearrange("p (t d) -> p t d", d=D),
                  in_=x1i.rearrange("(t p) d -> p t d", p=128)), 16)
            p.inc("dA", sy.dma_start(out=sWt[:, :], in_=WtT[:, :]), 16)
            p.inc("dA", sy.dma_start(out=sMx[:, :], in_=mxs[:, :]), 16)
            p.inc("dA", sy.dma_start(out=sBr[:, :], in_=biasr[:, :]), 16)
            p.inc("dA", sy.dma_start(out=sId[:, :], in_=identf[:, :]), 16)

        @B.vector
        def _(v):
            p.inc("VV", v.memset(sEps[:, :], 1e-5))
            p.inc("VV", v.memset(sOnR[:, :], 1.0))
            p.inc("VV", v.memset(sOnC[:, :], 1.0))
            p.wait(v, "dA", 16 * 5)
            p.inc("V1", v.reduce_sum(sAs[:, 0:1], sWt[:, :], axis=AX.X,
                                     apply_absolute_value=True))

        @B.tensor
        def _(t):
            p.wait(t, "V1", 1)
            p.inc("T1", t.transpose(PP[0][0:1, 0:128], sAs[:, 0:1], sId[:, :]))

        @B.scalar
        def _(s):
            p.wait(s, "T1", 1)
            p.inc("S1", s.copy(sR1[0:1, :], PP[0][0:1, 0:128]))

        @B.vector
        def _(v):
            p.wait(v, "S1", 1)
            p.inc("VV", v.reduce_sum(sS1[0:1, 0:1], sR1[0:1, :], axis=AX.X))
            p.wait(v, "VV", p.n("VV"))
            p.inc("VV", v.tensor_scalar_mul(sS1[0:1, 0:1], sS1[0:1, 0:1],
                                            1.0 / (1024.0 * 1024.0)))
            p.wait(v, "VV", p.n("VV"))
            p.inc("VV", v.tensor_scalar_mul(sS1[0:1, 1:2], sS1[0:1, 0:1], 0.5))
            p.inc("VV", v.reduce_max(sS1[0:1, 2:3], sMx[0:1, :], axis=AX.X))
            p.wait(v, "VV", p.n("VV"))
            p.inc("VV", v.tensor_scalar_mul(sS1[0:1, 3:4], sS1[0:1, 2:3], 1.0 / 127.0))
            p.wait(v, "VV", p.n("VV"))
            p.inc("VV", v.reciprocal(sS1[0:1, 4:5], sS1[0:1, 3:4]))
            p.wait(v, "VV", p.n("VV"))
            p.inc("V1", v.tensor_mul(sS1[0:1, 5:6], sS1[0:1, 0:1], sS1[0:1, 3:4]))
            p.wait(v, "V1", p.n("V1"))

        @B.tensor
        def _(t):
            p.wait(t, "V1", 2)
            p.inc("T1", t.matmul(PP[1][:, 0:1], sOnR[0:1, :], sS1[0:1, 4:5],
                                 start=True, stop=True))
            p.inc("T1", t.matmul(PP[2][:, 0:1], sOnR[0:1, :], sS1[0:1, 1:2],
                                 start=True, stop=True))
            p.inc("T1", t.matmul(PP[3][:, 0:1], sOnR[0:1, :], sS1[0:1, 5:6],
                                 start=True, stop=True))
            p.inc("T1", t.matmul(PP[4][:, 0:512], sOnR[0:1, :], sBr[0:1, 0:512],
                                 start=True, stop=True))
            p.inc("T1", t.matmul(PP[5][:, 0:512], sOnR[0:1, :], sBr[0:1, 512:1024],
                                 start=True, stop=True))

        @B.scalar
        def _(s):
            p.wait(s, "T1", 6)
            p.inc("S1", s.copy(sCl[:, 0:1], PP[1][:, 0:1]))
            p.inc("S1", s.copy(sCl[:, 1:2], PP[2][:, 0:1]))
            p.inc("S1", s.copy(sCl[:, 2:3], PP[3][:, 0:1]))
            p.inc("S1", s.copy(sBt[:, 0:512], PP[4][:, 0:512]))
            p.inc("S1", s.copy(sBt[:, 512:1024], PP[5][:, 0:512]))

        @B.vector
        def _(v):
            p.wait(v, "S1", 6)
            p.inc("VV", v.tensor_scalar_mul(sCl[:, 3:4], sCl[:, 1:2], -1.0))
            p.wait(v, "VV", p.n("VV"))
            p.inc("VV", v.tensor_scalar(sQW[:, :], sWt[:, :], sCl[:, 1:2], None,
                                        ALU.is_gt))
            p.inc("VV", v.tensor_scalar(sQW2[:, :], sWt[:, :], sCl[:, 3:4], None,
                                        ALU.is_lt))
            p.wait(v, "VV", p.n("VV"))
            p.inc("Vqw", v.tensor_sub(sQW[:, :], sQW[:, :], sQW2[:, :]))

        def emit_quant(v, j):
            pa = j % 2
            p.inc("VV", v.tensor_scalar(sQ3[:, pa * D:(pa + 1) * D],
                                        sX1[:, j * D:(j + 1) * D],
                                        sCl[:, 0:1], MAGIC, ALU.mult, ALU.add))
            p.wait(v, "VV", p.n("VV"))
            p.inc("VV", v.tensor_scalar(sQ3[:, pa * D:(pa + 1) * D],
                                        sQ3[:, pa * D:(pa + 1) * D],
                                        -MAGIC, -128.0, ALU.add, ALU.max))
            p.wait(v, "VV", p.n("VV"))
            p.inc("Vq", v.tensor_scalar(sQ3[:, pa * D:(pa + 1) * D],
                                        sQ3[:, pa * D:(pa + 1) * D],
                                        127.0, None, ALU.min))

        def emit_post(v, j):
            pa = j % 2
            p.wait(v, "Tmm", j + 1)
            p.inc("VV", v.scalar_tensor_tensor(sQ3[:, pa * D: pa * D + 512],
                                               PP[4][:, 0:512], sCl[:, 2:3],
                                               sBt[:, 0:512], ALU.mult, ALU.add))
            p.inc("VV", v.scalar_tensor_tensor(sQ3[:, pa * D + 512: pa * D + 1024],
                                               PP[5][:, 0:512], sCl[:, 2:3],
                                               sBt[:, 512:1024], ALU.mult, ALU.add))
            p.wait(v, "VV", p.n("VV"))
            p.inc("Vy", v.tensor_add(sQ3[:, pa * D:(pa + 1) * D],
                                     sQ3[:, pa * D:(pa + 1) * D],
                                     sX1[:, j * D:(j + 1) * D]))
            p.wait(v, "Vy", j + 1)
            p.inc("VV", v.reduce_sum(sSm[:, pa:pa + 1],
                                     sQ3[:, pa * D:(pa + 1) * D], axis=AX.X))
            p.wait(v, "SC5", 3 * j + 1)
            p.inc("Vsq", v.reduce_sum(sSm[:, 2 + pa:3 + pa], sSq[:, :], axis=AX.X))
            p.wait(v, "Vsq", j + 1)
            p.inc("VV", v.tensor_scalar_mul(sSm[:, pa:pa + 1], sSm[:, pa:pa + 1],
                                            1.0 / D))
            p.wait(v, "VV", p.n("VV"))
            p.inc("VV", v.tensor_mul(sSm[:, 4 + pa:5 + pa], sSm[:, pa:pa + 1],
                                     sSm[:, pa:pa + 1]))
            p.wait(v, "VV", p.n("VV"))
            p.inc("VV", v.tensor_scalar(sSm[:, 2 + pa:3 + pa], sSm[:, 2 + pa:3 + pa],
                                        1.0 / D, None, ALU.mult))
            p.wait(v, "VV", p.n("VV"))
            p.inc("Vmu", v.tensor_sub(sSm[:, 2 + pa:3 + pa], sSm[:, 2 + pa:3 + pa],
                                      sSm[:, 4 + pa:5 + pa]))
            p.wait(v, "SC5", 3 * j + 3)
            if j >= 2:
                p.wait(v, f"dO{j % 2}", 16 * (j // 2))
                p.wait(v, "Tcs", j - 1)
            p.inc("Vx2", v.tensor_scalar(sX2[:, pa * D:(pa + 1) * D],
                                         sQ3[:, pa * D:(pa + 1) * D],
                                         sSm[:, pa:pa + 1], sSm[:, 6 + pa:7 + pa],
                                         ALU.subtract, ALU.mult))

        @B.vector
        def _(v):
            p.wait(v, "S1", 6)
            emit_quant(v, 0)
            emit_quant(v, 1)
            for j in range(NT):
                emit_post(v, j)
                if j + 2 < NT:
                    p.wait(v, "Vx2", j + 1)
                    emit_quant(v, j + 2)

        @B.tensor
        def _(t):
            p.wait(t, "Vqw", 1)
            p.wait(t, "S1", 6)
            for j in range(NT):
                pa = j % 2
                p.wait(t, "Vq", j + 1)
                for c in range(8):
                    idx = 8 * j + c
                    if idx >= 4:
                        p.wait(t, "Sqt", idx - 3)
                    p.inc("Ttr", t.transpose(PP[idx % 4][:, 0:128],
                                             sQ3[:, pa * D + 128 * c: pa * D + 128 * (c + 1)],
                                             sId[:, :]))
                p.wait(t, "Sqt", 8 * (j + 1))
                if j >= 1:
                    p.wait(t, "Vy", j)
                for hf in range(2):
                    last = None
                    for kc in range(8):
                        last = t.matmul(PP[4 + hf][:, 0:512],
                                        sQT[:, pa * D + 128 * kc: pa * D + 128 * (kc + 1)],
                                        sQW[:, kc * 1024 + 512 * hf: kc * 1024 + 512 * (hf + 1)],
                                        start=(kc == 0), stop=(kc == 7))
                p.inc("Tmm", last)
                if j >= 1:
                    p.wait(t, "Vx2", j)
                    t.matmul(PP[6][0:1, 0:512], sOnC[:, 0:1],
                             sX2[:, ((j - 1) % 2) * D:((j - 1) % 2) * D + 512],
                             start=(j == 1), stop=False)
                    last = t.matmul(PP[7][0:1, 0:512], sOnC[:, 0:1],
                                    sX2[:, ((j - 1) % 2) * D + 512:((j - 1) % 2) * D + 1024],
                                    start=(j == 1), stop=False)
                    p.inc("Tcs", last)
            p.wait(t, "Vx2", NT)
            t.matmul(PP[6][0:1, 0:512], sOnC[:, 0:1],
                     sX2[:, ((NT - 1) % 2) * D:((NT - 1) % 2) * D + 512],
                     start=False, stop=True)
            last = t.matmul(PP[7][0:1, 0:512], sOnC[:, 0:1],
                            sX2[:, ((NT - 1) % 2) * D + 512:((NT - 1) % 2) * D + 1024],
                            start=False, stop=True)
            p.inc("Tcs", last)

        @B.scalar
        def _(s):
            for j in range(NT):
                pa = j % 2
                for c in range(8):
                    idx = 8 * j + c
                    p.wait(s, "Ttr", idx + 1)
                    p.inc("Sqt", s.copy(sQT[:, pa * D + 128 * c: pa * D + 128 * (c + 1)],
                                        PP[idx % 4][:, 0:128]))
                p.wait(s, "Vy", j + 1)
                if j >= 1:
                    p.wait(s, "Vsq", j)
                p.inc("SC5", s.activation(sSq[:, :], sQ3[:, pa * D:(pa + 1) * D],
                                          AF.Square))
                p.wait(s, "Vmu", j + 1)
                p.inc("SC5", s.activation(sSm[:, 4 + pa:5 + pa], sSm[:, 2 + pa:3 + pa],
                                          AF.Ln, bias=sEps[:, 0:1]))
                p.wait(s, "SC5", p.n("SC5"))
                p.inc("SC5", s.activation(sSm[:, 6 + pa:7 + pa], sSm[:, 4 + pa:5 + pa],
                                          AF.Exp, scale=-0.5))

        @B.sync
        def _(sy):
            for j in range(NT):
                p.wait(sy, "Vx2", j + 1)
                p.inc(f"dO{j % 2}", sy.dma_start(out=x2s[128 * j:128 * (j + 1), :],
                      in_=sX2[:, (j % 2) * D:(j % 2 + 1) * D]), 16)

        @B.scalar
        def _(s):
            p.wait(s, "Tcs", NT)
            p.inc("Scs", s.copy(sCS[0:1, 0:512], PP[6][0:1, 0:512]))
            p.inc("Scs", s.copy(sCS[0:1, 512:1024], PP[7][0:1, 0:512]))

        @B.sync
        def _(sy):
            p.wait(sy, "Scs", 2)
            p.inc("dOc", sy.dma_start(out=cs[:, :], in_=sCS[0:1, :]), 16)
            p.wait(sy, "dO0", 8 * 16)
            p.wait(sy, "dO1", 8 * 16)
            p.wait(sy, "dOc", 16)

        B.flush()
    return nc


def stage_lb(inputs, x1_full):
    WtT = _stage_w(np.asarray(inputs["L0_ffn_weight"], np.float32).T)
    biasr = np.asarray(inputs["L0_ffn_bias"], np.float32).reshape(1, D)
    identf = np.eye(128, dtype=np.float32)
    maps = []
    for cid in range(NCORE):
        maps.append({
            "x1i": np.ascontiguousarray(x1_full[RPC * cid:RPC * (cid + 1), :]),
            "mxs": None,
            "WtT": WtT, "biasr": biasr, "identf": identf,
        })
    return maps



def build_lc():
    nc = bass.Bass("TRN2", target_bir_lowering=False)
    x2i = nc.dram_tensor("x2i", [RPC, D], F32, kind="ExternalInput").ap()
    x2h = nc.dram_tensor("x2h", [32, D], F32, kind="ExternalInput").ap()
    csall = nc.dram_tensor("csall", [8, D], F32, kind="ExternalInput").ap()
    prefm = nc.dram_tensor("prefm", [8, 1], F32, kind="ExternalInput").ap()
    Wdb = nc.dram_tensor("Wdb", [128, 62], F32, kind="ExternalInput").ap()
    Wgb = nc.dram_tensor("Wgb", [128, 62], F32, kind="ExternalInput").ap()
    bped = nc.dram_tensor("bped", [1, 496], F32, kind="ExternalInput").ap()
    bpeg = nc.dram_tensor("bpeg", [1, 496], F32, kind="ExternalInput").ap()
    sttr = nc.dram_tensor("sttr", [1, 496], F32, kind="ExternalInput").ap()
    rsel = nc.dram_tensor("rsel", [16, 496], F32, kind="ExternalInput").ap()
    WfcT = nc.dram_tensor("WfcT", [128, 8192], F32, kind="ExternalInput").ap()
    gmr = nc.dram_tensor("gmr", [1, D], F32, kind="ExternalInput").ap()
    btr = nc.dram_tensor("btr", [1, D], F32, kind="ExternalInput").ap()
    recipc = nc.dram_tensor("recipc", [128, 16], F32, kind="ExternalInput").ap()
    reciph = nc.dram_tensor("reciph", [32, 1], F32, kind="ExternalInput").ap()
    trif = nc.dram_tensor("trif", [128, 128], F32, kind="ExternalInput").ap()
    nsl = nc.dram_tensor("nsl", [32, 32], F32, kind="ExternalInput").ap()
    cmsk = nc.dram_tensor("cmsk", [128, 496], F32, kind="ExternalInput").ap()
    identf = nc.dram_tensor("identf", [128, 128], F32, kind="ExternalInput").ap()
    x3s = nc.dram_tensor("x3s", [RPC, D], F32, kind="ExternalOutput").ap()
    SLOT = 16 * 129 * 192
    wz = nc.dram_tensor("wz", [NT * SLOT], BF16)

    with ExitStack() as st:
        E = st.enter_context
        sX2b = E(nc.sbuf_tensor("sX2b", [128, NT * D], BF16))
        sX2r = E(nc.sbuf_tensor("sX2r", [128, 2 * D], F32))
        sX2h = E(nc.sbuf_tensor("sX2h", [32, D], F32))
        sCsA = E(nc.sbuf_tensor("sCsA", [8, D], F32))
        sCsM = sCsA
        sPfm = E(nc.sbuf_tensor("sPfm", [8, 1], F32))
        sRun = E(nc.sbuf_tensor("sRun", [1, D], F32))
        sHci = E(nc.sbuf_tensor("sHci", [32, D], BF16))
        sCiH = E(nc.sbuf_tensor("sCiH", [64, D], BF16))
        sCiF = E(nc.sbuf_tensor("sCiF", [128, 2 * D], F32))
        sCi = E(nc.sbuf_tensor("sCi", [128, NT * D], BF16))
        sCiLo = E(nc.sbuf_tensor("sCiLo", [64, 2 * D], BF16))
        sCiT = E(nc.sbuf_tensor("sCiT", [128, 8 * 128], BF16))
        sWdb = E(nc.sbuf_tensor("sWdb", [128, 62], BF16))
        sWgb = E(nc.sbuf_tensor("sWgb", [128, 62], BF16))
        sBpdT = E(nc.sbuf_tensor("sBpdT", [128, 496], F32))
        sBpgT = E(nc.sbuf_tensor("sBpgT", [128, 496], F32))
        sSttT = E(nc.sbuf_tensor("sSttT", [128, 496], F32))
        sG = E(nc.sbuf_tensor("sG", [128, 496], F32))
        sDy = E(nc.sbuf_tensor("sDy", [128, 496], F32))
        sEn = sDy
        sEs = E(nc.sbuf_tensor("sEs", [128, 16], F32))
        sErc = E(nc.sbuf_tensor("sErc", [128, 16], F32))
        sErT = E(nc.sbuf_tensor("sErT", [16, 128], F32))
        sRs = E(nc.sbuf_tensor("sRs", [16, 496], F32))
        sWall = E(nc.sbuf_tensor("sWall", [128, 496], F32))
        sCmsk = E(nc.sbuf_tensor("sCmsk", [128, 496], F32))
        sPad = E(nc.sbuf_tensor("sPad", [128, 2 * 16 * 192], BF16))
        sTab = E(nc.sbuf_tensor("sTab", [128, 2 * 256], BF16))
        sCLT = E(nc.sbuf_tensor("sCLT", [128, 2 * D], BF16))
        sWfc = E(nc.sbuf_tensor("sWfc", [128, 8192], BF16))
        sGmT = E(nc.sbuf_tensor("sGmT", [128, D], F32))
        sBtT = E(nc.sbuf_tensor("sBtT", [128, D], F32))
        sYv = E(nc.sbuf_tensor("sYv", [128, D], F32))
        sXln = E(nc.sbuf_tensor("sXln", [128, D], F32))
        sSq = E(nc.sbuf_tensor("sSq", [128, D], F32))
        sX3 = E(nc.sbuf_tensor("sX3", [128, 2 * D], F32))
        sRc = E(nc.sbuf_tensor("sRc", [128, 16], F32))
        sRh = E(nc.sbuf_tensor("sRh", [32, 1], F32))
        sTrib = E(nc.sbuf_tensor("sTrib", [128, 128], BF16))
        sNsl = E(nc.sbuf_tensor("sNsl", [32, 32], F32))
        sId = E(nc.sbuf_tensor("sId", [128, 128], F32))
        sIdB = E(nc.sbuf_tensor("sIdB", [128, 128], BF16))
        sOnR = E(nc.sbuf_tensor("sOnR", [1, 128], F32))
        sOnC = E(nc.sbuf_tensor("sOnC", [128, 1], F32))
        sOnCb = E(nc.sbuf_tensor("sOnCb", [128, 1], BF16))
        sZro = E(nc.sbuf_tensor("sZro", [128, 1032], BF16))
        sSm = E(nc.sbuf_tensor("sSm", [128, 8], F32))
        sEp6 = E(nc.sbuf_tensor("sEp6", [128, 1], F32))
        sEp5 = E(nc.sbuf_tensor("sEp5", [128, 1], F32))
        PF = [E(nc.psum_tensor(f"PF{i}", [128, 512], F32)) for i in range(6)]
        PB = [E(nc.psum_tensor(f"PB{i}", [128, 1024], BF16)) for i in range(2)]

        p = Prog(nc, st)
        B = DeferredBlock(p.block)

        @B.sync
        def _(sy):
            for dst, srcx in ((sX2h, x2h), (sCsA, csall), (sPfm, prefm),
                              (sBpdT[0:1, :], bped), (sBpgT[0:1, :], bpeg),
                              (sSttT[0:1, :], sttr), (sRs, rsel),
                              (sGmT[0:1, :], gmr), (sBtT[0:1, :], btr),
                              (sRc, recipc), (sRh, reciph),
                              (sNsl, nsl), (sCmsk, cmsk), (sId, identf)):
                p.inc("dA", sy.dma_start(out=dst[:, :] if dst.shape[0] != 1 else dst,
                                         in_=srcx[:, :]), 16)

        @B.gpsimd
        def _(g):
            p.inc("dW", g.dma_start(out=sWdb[:, :], in_=Wdb[:, :]), 16)
            p.inc("dW", g.dma_start(out=sWgb[:, :], in_=Wgb[:, :]), 16)
            p.inc("dW", g.dma_start(out=sWfc[:, :], in_=WfcT[:, :]), 16)
            p.inc("dW", g.dma_start(out=sIdB[:, :], in_=identf[:, :]), 16)
            p.inc("dW", g.dma_start(out=sTrib[:, :], in_=trif[:, :]), 16)
            p.inc("dXb", g.dma_start(
                out=sX2b[:, :].rearrange("p (t d) -> p t d", d=D),
                in_=x2i.rearrange("(t p) d -> p t d", p=128)), 16)

        @B.vector
        def _(v):
            p.inc("VV", v.memset(sEp6[:, :], 1e-6))
            p.inc("VV", v.memset(sEp5[:, :], 1e-5))
            p.inc("VV", v.memset(sOnR[:, :], 1.0))
            p.inc("VV", v.memset(sOnC[:, :], 1.0))
            p.inc("VV", v.memset(sOnCb[:, :], 1.0))
            p.inc("V0", v.memset(sZro[:, :], 0.0))

        @B.gpsimd
        def _(g):
            p.inc("G1", g.memset(sCiH[:, :], 0.0))

        @B.tensor
        def _(t):
            p.wait(t, "dA", 14 * 16)
            p.wait(t, "VV", 5)
            p.inc("T0", t.matmul(PF[0][:, 0:496], sOnR[0:1, :], sBpdT[0:1, :],
                                 start=True, stop=True))
            p.inc("T0", t.matmul(PF[1][:, 0:496], sOnR[0:1, :], sBpgT[0:1, :],
                                 start=True, stop=True))
            p.inc("T0", t.matmul(PF[2][:, 0:496], sOnR[0:1, :], sSttT[0:1, :],
                                 start=True, stop=True))
            p.inc("T0", t.matmul(PF[3][:, 0:512], sOnR[0:1, :], sGmT[0:1, 0:512],
                                 start=True, stop=True))
            p.inc("T0", t.matmul(PF[4][:, 0:512], sOnR[0:1, :], sGmT[0:1, 512:1024],
                                 start=True, stop=True))
            p.inc("T0", t.matmul(PF[5][:, 0:512], sOnR[0:1, :], sBtT[0:1, 0:512],
                                 start=True, stop=True))

        @B.scalar
        def _(s):
            p.wait(s, "T0", 6)
            p.inc("S0", s.copy(sBpdT[:, :], PF[0][:, 0:496]))
            p.inc("S0", s.copy(sBpgT[:, :], PF[1][:, 0:496]))
            p.inc("S0", s.copy(sSttT[:, :], PF[2][:, 0:496]))
            p.inc("S0", s.copy(sGmT[:, 0:512], PF[3][:, 0:512]))
            p.inc("S0", s.copy(sGmT[:, 512:1024], PF[4][:, 0:512]))
            p.inc("S0", s.copy(sBtT[:, 0:512], PF[5][:, 0:512]))

        @B.tensor
        def _(t):
            p.wait(t, "S0", 6)
            p.inc("T0", t.matmul(PF[0][:, 0:512], sOnR[0:1, :], sBtT[0:1, 512:1024],
                                 start=True, stop=True))

        @B.scalar
        def _(s):
            p.wait(s, "T0", 7)
            p.inc("S0", s.copy(sBtT[:, 512:1024], PF[0][:, 0:512]))

        @B.vector
        def _(v):
            p.wait(v, "dA", 14 * 16)
            p.inc("V1", v.tensor_scalar_mul(sCsA[0:8, :], sCsA[0:8, :], sPfm[0:8, 0:1]))

        @B.tensor
        def _(t):
            p.wait(t, "V1", 1)
            p.wait(t, "S0", 7)
            for hf in range(2):
                p.inc("T1", t.matmul(PF[hf][0:1, 0:512], sOnC[0:8, 0:1],
                                     sCsM[0:8, 512 * hf:512 * (hf + 1)],
                                     start=True, stop=True))

        @B.scalar
        def _(s):
            p.wait(s, "T1", 2)
            for hf in range(2):
                p.inc("Sof", s.copy(sRun[0:1, 512 * hf:512 * (hf + 1)],
                                    PF[hf][0:1, 0:512]))

        @B.tensor
        def _(t):
            p.wait(t, "Sof", 2)
            for hf in range(2):
                t.matmul(PF[hf][0:32, 0:512], sNsl[0:32, :],
                         sX2h[0:32, 512 * hf:512 * (hf + 1)], start=True, stop=False)
                p.inc("Thl", t.matmul(PF[hf][0:32, 0:512], sOnR[0:1, 0:32],
                                      sRun[0:1, 512 * hf:512 * (hf + 1)],
                                      start=False, stop=True))

        @B.scalar
        def _(s):
            p.wait(s, "Thl", 2)
            for hf in range(2):
                p.inc("Shl", s.activation(sHci[0:32, 512 * hf:512 * (hf + 1)],
                                          PF[hf][0:32, 0:512], AF.Copy,
                                          scale=sRh[0:32, 0:1]))

        @B.gpsimd
        def _(g):
            p.wait(g, "Shl", 2)
            p.wait(g, "G1", 1)
            p.inc("dHl", g.dma_start(out=sCiH[32:64, :], in_=sHci[0:32, :]), 16)

        def emit_erec(t, jj):
            it = min(jj + 1, NT - 1)
            p.wait(t, "Ves", jj + 1)
            p.wait(t, "Sct", 8 * it + 5)
            p.inc("Tet", t.transpose(PF[0][0:16, 0:128], sErc[:, 0:16], sId[:, :]))
            p.wait(t, "Set", jj + 1)
            p.wait(t, "Sct", 8 * it + 6)
            if jj >= 1:
                p.wait(t, "Vw", jj + 1)
            p.inc("Ter", t.matmul(PF[1][:, 0:496], sErT[0:16, :], sRs[0:16, :],
                                  start=True, stop=True))

        @B.tensor
        def _(t):
            for j in range(NT):
                pa = j % 2
                p.wait(t, "Vrun", j)
                if j == 0:
                    p.wait(t, "Shl", 2)
                    p.wait(t, "dXb", 16)
                    p.wait(t, "dW", 80)
                elif j == 1:
                    p.wait(t, "Sct", 8 * (j - 1) + 6)
                else:
                    p.wait(t, "Set", j - 1)
                    p.wait(t, "Vw", j)
                for hf in range(2):
                    t.matmul(PF[hf][:, 0:512], sTrib[:, :],
                             sX2b[:, j * D + 512 * hf: j * D + 512 * (hf + 1)],
                             start=True, stop=False)
                    p.inc("Tcu", t.matmul(PF[hf][:, 0:512], sOnR[0:1, :],
                                          sRun[0:1, 512 * hf:512 * (hf + 1)],
                                          start=False, stop=True))
                for hf in range(2):
                    p.inc("Tco", t.matmul(PF[2 + hf][0:1, 0:512], sOnCb[:, 0:1],
                                          sX2b[:, j * D + 512 * hf: j * D + 512 * (hf + 1)],
                                          start=True, stop=True))
                p.wait(t, "Scu", 2 * (j + 1))
                if j >= 1:
                    p.wait(t, "Vc4", j)
                for c in range(8):
                    idx = 8 * j + c
                    if idx >= 4:
                        p.wait(t, "Sct", idx - 3)
                    bank = (0, 1, 4, 5)[c % 4]
                    p.inc("Ttr", t.transpose(PF[bank][:, 0:128],
                                             sCiF[:, pa * D + 128 * c: pa * D + 128 * (c + 1)],
                                             sId[:, :]))
                if j >= 1:
                    emit_erec(t, j - 1)
                p.wait(t, "Sct", 8 * (j + 1))
                if j == 0:
                    p.wait(t, "dW", 80)
                last_g = None
                for kc in range(8):
                    t.matmul(PF[4][:, 62 * kc:62 * (kc + 1)],
                             sCiT[:, 128 * kc:128 * (kc + 1)], sWdb[:, :],
                             start=(kc == 0), stop=(kc == 7))
                    last_g = t.matmul(PF[5][:, 62 * kc:62 * (kc + 1)],
                                      sCiT[:, 128 * kc:128 * (kc + 1)], sWgb[:, :],
                                      start=(kc == 0), stop=(kc == 7))
                p.inc("Tpc", last_g)
            emit_erec(t, NT - 1)

        @B.scalar
        def _(s):
            for j in range(NT):
                pa = j % 2
                p.wait(s, "Tcu", 2 * (j + 1))
                for hf in range(2):
                    p.inc("Scu", s.activation(
                        sCiF[:, pa * D + 512 * hf: pa * D + 512 * (hf + 1)],
                        PF[hf][:, 0:512], AF.Copy, scale=sRc[:, j:j + 1]))
                p.wait(s, "Scu", p.n("Scu"))
                p.inc("Sci", s.copy(sCi[:, j * D:(j + 1) * D],
                                    sCiF[:, pa * D:(pa + 1) * D]))
                for c in range(8):
                    idx = 8 * j + c
                    p.wait(s, "Ttr", idx + 1)
                    p.inc("Sct", s.copy(sCiT[:, 128 * c:128 * (c + 1)],
                                        PF[(0, 1, 4, 5)[c % 4]][:, 0:128]))
                if j >= 1:
                    p.wait(s, "Tet", j)
                    p.inc("Set", s.copy(sErT[0:16, :], PF[0][0:16, 0:128]))
                p.wait(s, "Vg", j + 1)
                p.inc("Sex", s.activation(sG[:, :], sG[:, :], AF.Exp, scale=-1.0))
                p.wait(s, "Sex", p.n("Sex"))
                p.inc("Sex", s.activation(sG[:, :], sG[:, :], AF.Ln,
                                          bias=sOnC[:, 0:1]))
                p.wait(s, "Sex", p.n("Sex"))
                p.inc("Sex", s.activation(sG[:, :], sG[:, :], AF.Exp, scale=-1.0))
                p.wait(s, "Ven", j + 1)
                p.inc("Sen", s.activation(sDy[:, :], sDy[:, :], AF.Exp))
            p.wait(s, "Tet", NT)
            p.inc("Set", s.copy(sErT[0:16, :], PF[0][0:16, 0:128]))

        @B.vector
        def _(v):
            for j in range(NT):
                p.wait(v, "Tco", 2 * (j + 1))
                v.tensor_add(sRun[0:1, 0:512], sRun[0:1, 0:512], PF[2][0:1, 0:512])
                p.inc("Vrun", v.tensor_add(sRun[0:1, 512:1024], sRun[0:1, 512:1024],
                                           PF[3][0:1, 0:512]))
                p.wait(v, "Tpc", j + 1)
                p.inc("Vg", v.tensor_add(sG[:, :], PF[5][:, 0:496], sBpgT[:, :]))
                p.inc("Vc4", v.tensor_add(sDy[:, :], PF[4][:, 0:496], sBpdT[:, :]))
                p.wait(v, "Vc4", j + 1)
                p.wait(v, "Sex", 3 * (j + 1))
                p.inc("VV", v.tensor_mul(sDy[:, :], sG[:, :], sDy[:, :]))
                p.wait(v, "VV", p.n("VV"))
                p.inc("Ven", v.tensor_add(sDy[:, :], sDy[:, :], sSttT[:, :]))
                p.wait(v, "Sen", j + 1)
                p.inc("VV", v.tensor_reduce(sEs[:, 0:16],
                      sDy[:, :].rearrange("p (h k) -> p h k", k=31),
                      axis=AX.X, op=ALU.add))
                p.wait(v, "VV", p.n("VV"))
                p.inc("Ves", v.reciprocal(sErc[:, 0:16], sEs[:, 0:16]))
                p.wait(v, "Ter", j + 1)
                if j >= 1:
                    p.wait(v, f"dSt{(j - 1) % 2}", 16 * ((j - 1) // 2 + 1))
                p.inc("Vw", v.tensor_mul(sWall[:, :], sDy[:, :], PF[1][:, 0:496]))
                if j == 0:
                    p.wait(v, "Vw", p.n("Vw"))
                    p.inc("Vw", v.tensor_mul(sWall[:, :], sWall[:, :], sCmsk[:, :]))
                p.wait(v, "Vw", p.n("Vw"))

        @B.gpsimd
        def _(g):
            p.wait(g, "V0", 1)
            for sl in range(16):
                for th in range(3):
                    p.inc("dZ", g.dma_start(
                        out=bass.AP(wz, sl * SLOT + th * 128 * 1032,
                                    [[1032, 128], [1, 1032]]),
                        in_=sZro[:, :]), 16)
            for j in range(NT):
                p.wait(g, "Vw", j + 2)
                if j == 0:
                    p.wait(g, "dZ", 16 * 48)
                p.inc(f"dSt{j % 2}", g.dma_start(
                    out=bass.AP(wz, j * SLOT + 192,
                                [[192, 128], [129 * 192, 16], [1, 31]]),
                    in_=sWall[:, :].rearrange("p (h k) -> p h k", k=31)), 16)
            for j in range(NT - 1):
                pa = j % 2
                if j >= 2:
                    p.wait(g, "Tbd", 16 * j)
                p.inc(f"dCl{pa}", g.dma_start(out=sCiLo[0:64, pa * D:(pa + 1) * D],
                      in_=sCi[64:128, j * D:(j + 1) * D]), 16)

        @B.sync
        def _(sy):
            for j in range(NT):
                p.wait(sy, f"dSt{j % 2}", 16 * (j // 2 + 1))
                if j >= 2:
                    p.wait(sy, "Ttb", 16 * (j - 1))
                p.inc(f"dSk{j % 2}", sy.dma_start(
                    out=sPad[:, (j % 2) * 3072:(j % 2 + 1) * 3072]
                        .rearrange("p (h c) -> p h c", c=192),
                    in_=bass.AP(wz, j * SLOT + 192 - 34,
                                [[191, 128], [129 * 192, 16], [1, 192]])), 16)
                if j >= 2:
                    p.wait(sy, "Vy2", j - 1)
                p.inc(f"dXr{j % 2}", sy.dma_start(
                    out=sX2r[:, (j % 2) * D:(j % 2 + 1) * D],
                    in_=x2i[128 * j:128 * (j + 1), :]), 16)
                if j >= 2:
                    p.wait(sy, "Vx3", j - 1)
                    p.inc(f"dO{j % 2}", sy.dma_start(
                        out=x3s[128 * (j - 2):128 * (j - 1), :],
                        in_=sX3[:, ((j - 2) % 2) * D:((j - 2) % 2 + 1) * D]), 16)
            for j in (NT - 2, NT - 1):
                p.wait(sy, "Vx3", j + 1)
                p.inc(f"dO{j % 2}", sy.dma_start(out=x3s[128 * j:128 * (j + 1), :],
                      in_=sX3[:, (j % 2) * D:(j % 2 + 1) * D]), 16)
            p.wait(sy, "dO0", 8 * 16)
            p.wait(sy, "dO1", 8 * 16)

        @B.tensor
        def _(t):
            for hi in range(NT * 16 + 2):
                if hi < NT * 16:
                    j, h = divmod(hi, 16)
                    pa = j % 2
                    if h == 0:
                        p.wait(t, f"dSk{pa}", 16 * (j // 2 + 1))
                        if j == 0:
                            p.wait(t, "dHl", 16)
                            p.wait(t, "Vrun", NT)
                            p.wait(t, "Vc4", NT)
                            p.wait(t, "Set", NT)
                            p.wait(t, "Vw", NT + 1)
                    if hi >= 2:
                        p.wait(t, "Stb", 2 * (hi - 1))
                    off = pa * 3072 + 192 * h
                    t.transpose(PB[h % 2][0:128, 0:128], sPad[:, off:off + 128],
                                sIdB[:, :])
                    p.inc("Ttb", t.transpose(PB[h % 2][0:128, 128:256],
                                             sPad[:, off + 64:off + 192], sIdB[:, :]))
                g = hi - 2
                if g >= 0:
                    jg, hg = divmod(g, 16)
                    pag = jg % 2
                    if hi >= NT * 16:
                        p.wait(t, "Stb", 2 * (g + 1))
                    if hg == 0 and jg >= 1:
                        p.wait(t, f"dCl{(jg - 1) % 2}", 16 * ((jg - 1) // 2 + 1))
                    if g >= 4:
                        gb = g - 4
                        if gb % 2 == 0:
                            p.wait(t, "SbdV", gb // 2 + 1)
                        else:
                            p.wait(t, "SbdS", gb // 2 + 1)
                    to = 256 * (hg % 2)
                    if jg == 0:
                        cia = sCiH[0:64, 64 * hg:64 * (hg + 1)]
                    else:
                        cia = sCiLo[0:64, ((jg - 1) % 2) * D + 64 * hg:
                                     ((jg - 1) % 2) * D + 64 * (hg + 1)]
                    t.matmul(PF[2 + hg % 4][0:64, 0:128], cia,
                             sTab[0:64, to:to + 128], start=True, stop=False)
                    p.inc("Tbd", t.matmul(
                        PF[2 + hg % 4][0:64, 0:128],
                        sCi[0:128, jg * D + 64 * hg: jg * D + 64 * (hg + 1)],
                        sTab[0:128, to + 128:to + 256], start=False, stop=True))
                    if hg == 15:
                        p.wait(t, "SbdV", 8 * (jg + 1))
                        p.wait(t, "SbdS", 8 * (jg + 1))
                        if jg >= 1:
                            p.wait(t, "Vv", jg)
                        for hf in range(2):
                            last = None
                            for dc in range(8):
                                last = t.matmul(
                                    PF[hf][:, 0:512],
                                    sCLT[:, pag * D + 128 * dc: pag * D + 128 * (dc + 1)],
                                    sWfc[:, dc * 1024 + 512 * hf: dc * 1024 + 512 * (hf + 1)],
                                    start=(dc == 0), stop=(dc == 7))
                        p.inc("Tfc", last)

        @B.scalar
        def _(s):
            def ln_scalar(j):
                p.wait(s, "Vv", j + 1)
                if j >= 1:
                    p.wait(s, "Vsq2", j)
                p.inc("SC6", s.activation(sSq[:, :], sYv[:, :], AF.Square))
                p.wait(s, "Vmu1", j + 1)
                p.inc("SC6", s.activation(sSm[:, 4:5], sSm[:, 2:3], AF.Ln,
                                          bias=sEp6[:, 0:1]))
                p.wait(s, "SC6", p.n("SC6"))
                p.inc("SC6", s.activation(sSm[:, 6:7], sSm[:, 4:5], AF.Exp,
                                          scale=-0.5))
                p.wait(s, "Vy2", j + 1)
                p.wait(s, "Vsq1", j + 1)
                p.inc("SC6", s.activation(sSq[:, :], sYv[:, :], AF.Square))
                p.wait(s, "Vmu2", j + 1)
                p.inc("SC6", s.activation(sSm[:, 5:6], sSm[:, 3:4], AF.Ln,
                                          bias=sEp5[:, 0:1]))
                p.wait(s, "SC6", p.n("SC6"))
                p.inc("SC6", s.activation(sSm[:, 7:8], sSm[:, 5:6], AF.Exp,
                                          scale=-0.5))

            for hi in range(NT * 16 + 2):
                if hi < NT * 16:
                    j, h = divmod(hi, 16)
                    p.wait(s, "Ttb", hi + 1)
                    if hi >= 2:
                        p.wait(s, "Tbd", hi - 1)
                    to = 256 * (h % 2)
                    p.inc("Stb", s.copy(sTab[0:128, to:to + 256],
                                        PB[h % 2][0:128, 0:256]), 2)
                g = hi - 2
                if g >= 0 and g % 2 == 1:
                    jg, hg = divmod(g, 16)
                    p.wait(s, "Tbd", g + 1)
                    p.inc("SbdS", s.copy(
                        sCLT[64:128,
                             (jg % 2) * D + (hg // 2) * 128:
                             (jg % 2) * D + (hg // 2) * 128 + 128],
                        PF[2 + hg % 4][0:64, 0:128]))
                if g >= 0 and g % 16 == 15 and g // 16 >= 1:
                    ln_scalar(g // 16 - 1)
            ln_scalar(NT - 1)

        @B.vector
        def _(v):
            def ln_vec(j):
                pa = j % 2
                p.wait(v, "Tfc", j + 1)
                p.wait(v, f"dXr{pa}", 16 * (j // 2 + 1))
                v.tensor_add(sYv[:, 0:512], PF[0][:, 0:512],
                             sX2r[:, pa * D: pa * D + 512])
                p.inc("Vv", v.tensor_add(sYv[:, 512:1024], PF[1][:, 0:512],
                                         sX2r[:, pa * D + 512: pa * D + 1024]))
                p.wait(v, "Vv", j + 1)
                p.inc("VV", v.reduce_sum(sSm[:, 0:1], sYv[:, :], axis=AX.X))
                p.wait(v, "SC6", 6 * j + 1)
                p.inc("Vsq1", v.reduce_sum(sSm[:, 2:3], sSq[:, :], axis=AX.X))
                p.wait(v, "Vsq1", j + 1)
                p.inc("VV", v.tensor_scalar_mul(sSm[:, 0:1], sSm[:, 0:1], 1.0 / D))
                p.wait(v, "VV", p.n("VV"))
                p.inc("VV", v.tensor_mul(sSm[:, 4:5], sSm[:, 0:1], sSm[:, 0:1]))
                p.wait(v, "VV", p.n("VV"))
                p.inc("VV", v.tensor_scalar(sSm[:, 2:3], sSm[:, 2:3], 1.0 / D,
                                            None, ALU.mult))
                p.wait(v, "VV", p.n("VV"))
                p.inc("Vmu1", v.tensor_sub(sSm[:, 2:3], sSm[:, 2:3], sSm[:, 4:5]))
                p.wait(v, "SC6", 6 * j + 3)
                p.inc("VV", v.tensor_scalar(sXln[:, :], sYv[:, :], sSm[:, 0:1],
                                            sSm[:, 6:7], ALU.subtract, ALU.mult))
                p.wait(v, "VV", p.n("VV"))
                p.inc("VV", v.tensor_mul(sXln[:, :], sXln[:, :], sGmT[:, :]))
                p.wait(v, "VV", p.n("VV"))
                p.inc("VV", v.tensor_add(sXln[:, :], sXln[:, :], sBtT[:, :]))
                p.wait(v, "VV", p.n("VV"))
                p.inc("Vy2", v.tensor_add(sYv[:, :], sXln[:, :],
                                          sX2r[:, pa * D:(pa + 1) * D]))
                p.wait(v, "Vy2", j + 1)
                p.inc("VV", v.reduce_sum(sSm[:, 1:2], sYv[:, :], axis=AX.X))
                p.wait(v, "SC6", 6 * j + 4)
                p.inc("Vsq2", v.reduce_sum(sSm[:, 3:4], sSq[:, :], axis=AX.X))
                p.wait(v, "Vsq2", j + 1)
                p.inc("VV", v.tensor_scalar_mul(sSm[:, 1:2], sSm[:, 1:2], 1.0 / D))
                p.wait(v, "VV", p.n("VV"))
                p.inc("VV", v.tensor_mul(sSm[:, 5:6], sSm[:, 1:2], sSm[:, 1:2]))
                p.wait(v, "VV", p.n("VV"))
                p.inc("VV", v.tensor_scalar(sSm[:, 3:4], sSm[:, 3:4], 1.0 / D,
                                            None, ALU.mult))
                p.wait(v, "VV", p.n("VV"))
                p.inc("Vmu2", v.tensor_sub(sSm[:, 3:4], sSm[:, 3:4], sSm[:, 5:6]))
                p.wait(v, "SC6", 6 * j + 6)
                if j >= 2:
                    p.wait(v, f"dO{pa}", 16 * (j // 2))
                p.inc("Vx3", v.tensor_scalar(sX3[:, pa * D:(pa + 1) * D], sYv[:, :],
                                             sSm[:, 1:2], sSm[:, 7:8],
                                             ALU.subtract, ALU.mult))
                p.wait(v, "Vx3", j + 1)

            for j in range(NT):
                pa = j % 2
                for hg in range(0, 16, 2):
                    g = 16 * j + hg
                    p.wait(v, "Tbd", g + 1)
                    p.inc("SbdV", v.tensor_scalar(
                        sCLT[0:64,
                             pa * D + (hg // 2) * 128:
                             pa * D + (hg // 2) * 128 + 128],
                        PF[2 + hg % 4][0:64, 0:128], 1.0, None, ALU.mult))
                if j >= 1:
                    ln_vec(j - 1)
            ln_vec(NT - 1)

        B.flush()
    return nc


def stage_lc(inputs, x2_full, cs_all, cid_halos):
    W_pe = np.asarray(inputs["L1_attn_W_pe"], np.float32)
    Wd = W_pe[:, 0:31]
    Wg = W_pe[:, 31:62]
    Wdb = np.zeros((128, 62), np.float32)
    Wdb[0:64, 0:31] = Wd
    Wdb[64:128, 31:62] = Wd
    Wgb = np.zeros((128, 62), np.float32)
    Wgb[0:64, 0:31] = Wg
    Wgb[64:128, 31:62] = Wg
    b_pe = np.asarray(inputs["L1_attn_b_pe"], np.float32)
    bped = np.tile(b_pe[0:31], 16).reshape(1, 496)
    bpeg = np.tile(b_pe[31:62], 16).reshape(1, 496)
    sttr = np.asarray(inputs["L1_attn_stt_energy"], np.float32).reshape(1, 496)
    rsel = np.zeros((16, 496), np.float32)
    for h in range(16):
        rsel[h, 31 * h:31 * (h + 1)] = 1.0
    WfcT = _stage_w(np.asarray(inputs["L1_attn_W_fc"], np.float32).T)
    gmr = np.asarray(inputs["L1_attn_gamma"], np.float32).reshape(1, D)
    btr = np.asarray(inputs["L1_attn_beta"], np.float32).reshape(1, D)
    trif = np.triu(np.ones((128, 128), np.float32))
    nsl = -np.tril(np.ones((32, 32), np.float32), -1)
    identf = np.eye(128, dtype=np.float32)
    maps = []
    for cid in range(NCORE):
        t0 = RPC * cid
        rc = np.zeros((128, 16), np.float32)
        for jj in range(16):
            rc[:, jj] = 1.0 / (t0 + 128 * jj + np.arange(128) + 1.0)
        if cid == 0:
            rh = np.zeros((32, 1), np.float32)
            x2hv = np.zeros((32, D), np.float32)
            cm = np.ones((128, 496), np.float32)
            pk = np.arange(128)[:, None]
            kk = np.tile(np.arange(31)[None, :], (1, 16)).reshape(1, 496)
            cm = (pk + kk >= 30).astype(np.float32)
        else:
            rh = (1.0 / (t0 - 32 + np.arange(32) + 1.0)).astype(np.float32).reshape(32, 1)
            x2hv = np.ascontiguousarray(x2_full[t0 - 32:t0, :])
            cm = np.ones((128, 496), np.float32)
        pf = (np.arange(8) < cid).astype(np.float32).reshape(8, 1)
        maps.append({
            "x2i": np.ascontiguousarray(x2_full[t0:t0 + RPC, :]),
            "x2h": x2hv, "csall": cs_all, "prefm": pf,
            "Wdb": Wdb, "Wgb": Wgb, "bped": bped, "bpeg": bpeg, "sttr": sttr,
            "rsel": rsel, "WfcT": WfcT, "gmr": gmr, "btr": btr,
            "recipc": rc, "reciph": rh, "trif": trif, "nsl": nsl,
            "cmsk": cm, "identf": identf,
        })
    return maps



def build_ld():
    nc = bass.Bass("TRN2", target_bir_lowering=False)
    x3i = nc.dram_tensor("x3i", [RPC, D], F32, kind="ExternalInput").ap()
    ufT = nc.dram_tensor("ufT", [128, 8192], F32, kind="ExternalInput").ap()
    uiT = nc.dram_tensor("uiT", [128, 8192], F32, kind="ExternalInput").ap()
    ucT = nc.dram_tensor("ucT", [128, 8192], F32, kind="ExternalInput").ap()
    wfT = nc.dram_tensor("wfT", [128, 8192], F32, kind="ExternalInput").ap()
    wiT = nc.dram_tensor("wiT", [128, 8192], F32, kind="ExternalInput").ap()
    wcT = nc.dram_tensor("wcT", [128, 8192], F32, kind="ExternalInput").ap()
    woT = nc.dram_tensor("woT", [128, 8192], F32, kind="ExternalInput").ap()
    uoT = nc.dram_tensor("uoT", [128, 8192], F32, kind="ExternalInput").ap()
    incc = nc.dram_tensor("incc", [128, 8], F32, kind="ExternalInput").ap()
    incr = nc.dram_tensor("incr", [1, D], F32, kind="ExternalInput").ap()
    brows = nc.dram_tensor("brows", [1, 4 * D], F32, kind="ExternalInput").ap()
    identf = nc.dram_tensor("identf", [128, 128], F32, kind="ExternalInput").ap()
    x4s = nc.dram_tensor("x4s", [RPC, D], F32, kind="ExternalOutput").ap()

    I32 = mybir.dt.int32

    with ExitStack() as st:
        E = st.enter_context
        sX3 = E(nc.sbuf_tensor("sX3", [128, NT * D], F32))
        sWf = E(nc.sbuf_tensor("sWf", [128, 8192], BF16))
        sWi = E(nc.sbuf_tensor("sWi", [128, 8192], BF16))
        sWc = E(nc.sbuf_tensor("sWc", [128, 8192], BF16))
        sWo = E(nc.sbuf_tensor("sWo", [128, 8192], BF16))
        sWu = E(nc.sbuf_tensor("sWu", [128, 8192], BF16))
        sX3T = E(nc.sbuf_tensor("sX3T", [128, 2 * D], BF16))
        sC1T = E(nc.sbuf_tensor("sC1T", [128, 2 * D], BF16))
        sC1 = E(nc.sbuf_tensor("sC1", [128, D], F32))
        sIt = E(nc.sbuf_tensor("sIt", [128, D], F32))
        sGt = E(nc.sbuf_tensor("sGt", [128, D], F32))
        sHt = sGt
        sC0t = E(nc.sbuf_tensor("sC0t", [128, D], F32))
        sIcc = E(nc.sbuf_tensor("sIcc", [128, 8], BF16))
        sIcr = E(nc.sbuf_tensor("sIcr", [1, D], F32))
        sRfb = E(nc.sbuf_tensor("sRfb", [1, 4 * D], BF16))
        sOnRb = E(nc.sbuf_tensor("sOnRb", [1, 128], BF16))
        sSq = E(nc.sbuf_tensor("sSq", [128, D], F32))
        sX4 = E(nc.sbuf_tensor("sX4", [128, 2 * D], F32))
        sId = E(nc.sbuf_tensor("sId", [128, 128], F32))
        sOnR = E(nc.sbuf_tensor("sOnR", [1, 128], F32))
        sSm = E(nc.sbuf_tensor("sSm", [128, 8], F32))
        sMgk = E(nc.sbuf_tensor("sMgk", [128, 1], I32))
        PF = [E(nc.psum_tensor(f"PF{i}", [128, 512], F32)) for i in range(8)]

        p = Prog(nc, st)
        B = DeferredBlock(p.block)

        @B.sync
        def _(sy):
            p.inc("dA", sy.dma_start(out=sX3[:, :].rearrange("p (t d) -> p t d", d=D),
                  in_=x3i.rearrange("(t p) d -> p t d", p=128)), 16)
            p.inc("dA", sy.dma_start(out=sIcr[:, :], in_=incr[:, :]), 16)
            p.inc("dA", sy.dma_start(out=sId[:, :], in_=identf[:, :]), 16)

        @B.vector
        def _(v):
            p.inc("VV", v.memset(sOnR[:, :], 1.0))
            p.inc("VV", v.memset(sOnRb[:, :], 1.0))
            p.inc("VV", v.memset(sMgk[:, :], 0x5f3759df))

        @B.gpsimd
        def _(g):
            p.inc("dIc", g.dma_start(out=sIcc[:, :], in_=incc[:, :]), 16)
            p.inc("dBr", g.dma_start(out=sRfb[:, :], in_=brows[:, :]), 16)
            p.inc("dU0", g.dma_start(out=sWf[:, :], in_=ufT[:, :]), 16)
            p.wait(g, "Tmv", 2)
            p.inc("dU1", g.dma_start(out=sWf[:, :], in_=uiT[:, :]), 16)
            p.wait(g, "Tmv", 4)
            p.inc("dU2", g.dma_start(out=sWf[:, :], in_=ucT[:, :]), 16)
            p.wait(g, "Tmv", 6)
            p.inc("dW0", g.dma_start(out=sWf[:, :], in_=wfT[:, :]), 16)
            p.inc("dW1", g.dma_start(out=sWi[:, :], in_=wiT[:, :]), 16)
            p.inc("dW2", g.dma_start(out=sWc[:, :], in_=wcT[:, :]), 16)
            p.inc("dW3", g.dma_start(out=sWo[:, :], in_=woT[:, :]), 16)
            p.inc("dW4", g.dma_start(out=sWu[:, :], in_=uoT[:, :]), 16)

        @B.tensor
        def _(t):
            p.wait(t, "dA", 16 * 3)
            p.wait(t, "dIc", 16)
            p.wait(t, "dBr", 16)
            p.wait(t, "VV", 3)
            p.inc("Tc0", t.matmul(PF[2][:, 0:512], sOnR[0:1, :], sIcr[0:1, 0:512],
                                  start=True, stop=True))
            p.inc("Tc0", t.matmul(PF[3][:, 0:512], sOnR[0:1, :], sIcr[0:1, 512:1024],
                                  start=True, stop=True))
            for u in range(3):
                p.wait(t, f"dU{u}", 16)
                if u > 0:
                    p.wait(t, "Smv", 2 * u)
                for hf in range(2):
                    for kc in range(8):
                        t.matmul(PF[4 + hf][0:1, 0:512], sIcc[:, kc:kc + 1],
                                 sWf[:, kc * 1024 + 512 * hf: kc * 1024 + 512 * (hf + 1)],
                                 start=(kc == 0), stop=False)
                    p.inc("Tmv", t.matmul(
                        PF[4 + hf][0:1, 0:512], sOnRb[0:1, 0:1],
                        sRfb[0:1, u * D + 512 * hf: u * D + 512 * (hf + 1)],
                        start=False, stop=True))

        @B.scalar
        def _(s):
            p.wait(s, "Tc0", 2)
            p.inc("Sc0", s.copy(sC0t[:, 0:512], PF[2][:, 0:512]))
            p.inc("Sc0", s.copy(sC0t[:, 512:1024], PF[3][:, 0:512]))
            for u in range(3):
                for hf in range(2):
                    p.wait(s, "Tmv", 2 * u + hf + 1)
                    p.inc("Smv", s.copy(
                        sRfb[0:1, u * D + 512 * hf: u * D + 512 * (hf + 1)],
                        PF[4 + hf][0:1, 0:512]))

        trn = [0]

        def emit_x3t(t, j):
            for c in range(8):
                k = trn[0]
                if k >= 4:
                    p.wait(t, "Str", k - 3)
                p.inc("Ttr", t.transpose(PF[k % 4][:, 0:128],
                                         sX3[:, j * D + 128 * c: j * D + 128 * (c + 1)],
                                         sId[:, :]))
                trn[0] += 1

        def emit_c1t(t, j):
            for c in range(8):
                k = trn[0]
                if k >= 4:
                    p.wait(t, "Str", k - 3)
                p.inc("Ttr", t.transpose(PF[k % 4][:, 0:128],
                                         sC1[:, 128 * c:128 * (c + 1)], sId[:, :]))
                trn[0] += 1

        @B.tensor
        def _(t):
            p.wait(t, "Smv", 6)
            p.wait(t, "Sc0", 2)
            for nm in ("dW0", "dW1", "dW2", "dW3", "dW4"):
                p.wait(t, nm, 16)
            emit_x3t(t, 0)
            for j in range(NT):
                pa = j % 2
                p.wait(t, "Str", 16 * j + 8)
                if j >= 1:
                    p.wait(t, "Sg", 6 * j)
                for gi, wsb in enumerate((sWf, sWi, sWc)):
                    b0 = 4 if gi % 2 == 0 else 6
                    if gi == 1 and j >= 1:
                        p.wait(t, "Sh", 2 * j)
                    if gi == 2:
                        p.wait(t, "Sg", 6 * j + 2)
                    for hf in range(2):
                        for kc in range(8):
                            t.matmul(PF[b0 + hf][:, 0:512],
                                     sX3T[:, pa * D + 128 * kc: pa * D + 128 * (kc + 1)],
                                     wsb[:, kc * 1024 + 512 * hf: kc * 1024 + 512 * (hf + 1)],
                                     start=(kc == 0), stop=False)
                        p.inc("Tg", t.matmul(
                            PF[b0 + hf][:, 0:512], sOnRb[0:1, :],
                            sRfb[0:1, gi * D + 512 * hf: gi * D + 512 * (hf + 1)],
                            start=False, stop=True))
                if j + 1 < NT:
                    emit_x3t(t, j + 1)
                p.wait(t, "Vc1", j + 1)
                emit_c1t(t, j)
                p.wait(t, "Str", trn[0])
                p.wait(t, "Sg", 6 * j + 4)
                for hf in range(2):
                    for kc in range(8):
                        t.matmul(PF[6 + hf][:, 0:512],
                                 sX3T[:, pa * D + 128 * kc: pa * D + 128 * (kc + 1)],
                                 sWo[:, kc * 1024 + 512 * hf: kc * 1024 + 512 * (hf + 1)],
                                 start=(kc == 0), stop=False)
                    for kc in range(8):
                        t.matmul(PF[6 + hf][:, 0:512],
                                 sC1T[:, pa * D + 128 * kc: pa * D + 128 * (kc + 1)],
                                 sWu[:, kc * 1024 + 512 * hf: kc * 1024 + 512 * (hf + 1)],
                                 start=False, stop=False)
                    p.inc("Th", t.matmul(
                        PF[6 + hf][:, 0:512], sOnRb[0:1, :],
                        sRfb[0:1, 3 * D + 512 * hf: 3 * D + 512 * (hf + 1)],
                        start=False, stop=True))

        @B.scalar
        def _(s):
            etn = [0]

            def evac_tr(dst_base, pa_, j_, kind):
                for c in range(8):
                    k = etn[0]
                    p.wait(s, "Ttr", k + 1)
                    if kind == "x":
                        dst = sX3T[:, pa_ * D + 128 * c: pa_ * D + 128 * (c + 1)]
                    else:
                        dst = sC1T[:, pa_ * D + 128 * c: pa_ * D + 128 * (c + 1)]
                    p.inc("Str", s.copy(dst, PF[k % 4][:, 0:128]))
                    etn[0] += 1

            evac_tr(None, 0, 0, "x")
            for j in range(NT):
                pa = j % 2
                for gi, (buf, fn) in enumerate(((sC1, AF.Sigmoid), (sIt, AF.Sigmoid),
                                                (sGt, AF.Tanh))):
                    b0 = 4 if gi % 2 == 0 else 6
                    for hf in range(2):
                        p.wait(s, "Tg", 6 * j + 2 * gi + hf + 1)
                        if j >= 1 and gi == 0:
                            p.wait(s, "Vc1", j)
                        if j >= 1 and gi == 2 and hf == 0:
                            p.wait(s, "Vx4", j)
                        p.inc("Sg", s.activation(buf[:, 512 * hf:512 * (hf + 1)],
                                                 PF[b0 + hf][:, 0:512], fn))
                if j + 1 < NT:
                    if j >= 1:
                        p.wait(s, "Th", 2 * j)
                    evac_tr(None, (j + 1) % 2, j + 1, "x")
                evac_tr(None, pa, j, "c")
                for hf in range(2):
                    p.wait(s, "Th", 2 * j + hf + 1)
                    if j >= 1 and hf == 0:
                        p.wait(s, "Vx4", j)
                    p.inc("Sh", s.activation(sHt[:, 512 * hf:512 * (hf + 1)],
                                             PF[6 + hf][:, 0:512], AF.Tanh))
                p.wait(s, "Vy4", j + 1)
                if j >= 1:
                    p.wait(s, "Vsq4", j)
                p.inc("Ssq4", s.activation(sSq[:, :], sHt[:, :], AF.Square))

        @B.vector
        def _(v):
            for j in range(NT):
                pa = j % 2
                p.wait(v, "Sg", 6 * (j + 1))
                p.inc("VV", v.tensor_mul(sC1[:, :], sC1[:, :], sC0t[:, :]))
                p.inc("VV", v.tensor_mul(sIt[:, :], sIt[:, :], sGt[:, :]))
                p.wait(v, "VV", p.n("VV"))
                p.inc("Vc1", v.tensor_add(sC1[:, :], sC1[:, :], sIt[:, :]))
                p.wait(v, "Sh", 2 * (j + 1))
                p.inc("Vy4", v.tensor_add(sHt[:, :], sHt[:, :],
                                          sX3[:, j * D:(j + 1) * D]))
                p.wait(v, "Vy4", j + 1)
                p.inc("VV", v.reduce_sum(sSm[:, 0:1], sHt[:, :], axis=AX.X))
                p.wait(v, "Ssq4", j + 1)
                p.inc("Vsq4", v.reduce_sum(sSm[:, 2:3], sSq[:, :], axis=AX.X))
                p.wait(v, "Vsq4", j + 1)
                p.inc("VV", v.tensor_scalar_mul(sSm[:, 0:1], sSm[:, 0:1], 1.0 / D))
                p.wait(v, "VV", p.n("VV"))
                p.inc("VV", v.tensor_mul(sSm[:, 4:5], sSm[:, 0:1], sSm[:, 0:1]))
                p.wait(v, "VV", p.n("VV"))
                p.inc("VV", v.tensor_scalar(sSm[:, 2:3], sSm[:, 2:3], 1.0 / D, None,
                                            ALU.mult))
                p.wait(v, "VV", p.n("VV"))
                p.inc("VV", v.tensor_sub(sSm[:, 2:3], sSm[:, 2:3], sSm[:, 4:5]))
                p.wait(v, "VV", p.n("VV"))
                p.inc("VV", v.tensor_scalar(sSm[:, 2:3], sSm[:, 2:3], 1e-5, None,
                                            ALU.add))
                p.wait(v, "VV", p.n("VV"))
                vcol = sSm[:, 2:3]
                r = sSm[:, 6:7]
                t1 = sSm[:, 5:6]
                p.inc("VV", v.tensor_scalar(r.bitcast(I32), vcol.bitcast(I32), 1, None,
                                            ALU.arith_shift_right))
                p.wait(v, "VV", p.n("VV"))
                p.inc("VV", v.tensor_sub(r.bitcast(I32), sMgk[:, 0:1], r.bitcast(I32)))
                for _i in range(3):
                    p.wait(v, "VV", p.n("VV"))
                    p.inc("VV", v.tensor_mul(t1, r, r))
                    p.wait(v, "VV", p.n("VV"))
                    p.inc("VV", v.tensor_mul(t1, t1, vcol))
                    p.wait(v, "VV", p.n("VV"))
                    p.inc("VV", v.tensor_scalar(t1, t1, -0.5, 1.5, ALU.mult, ALU.add))
                    p.wait(v, "VV", p.n("VV"))
                    p.inc("VV", v.tensor_mul(r, r, t1))
                p.wait(v, "VV", p.n("VV"))
                if j >= 2:
                    p.wait(v, f"dO{pa}", 16 * (j // 2))
                p.inc("Vx4", v.tensor_scalar(sX4[:, pa * D:(pa + 1) * D], sHt[:, :],
                                             sSm[:, 0:1], sSm[:, 6:7],
                                             ALU.subtract, ALU.mult))

        @B.sync
        def _(sy):
            for j in range(NT):
                p.wait(sy, "Vx4", j + 1)
                p.inc(f"dO{j % 2}", sy.dma_start(out=x4s[128 * j:128 * (j + 1), :],
                      in_=sX4[:, (j % 2) * D:(j % 2 + 1) * D]), 16)

        @B.sync
        def _(sy):
            p.wait(sy, "dO0", 8 * 16)
            p.wait(sy, "dO1", 8 * 16)

        B.flush()
    return nc


def stage_ld(inputs, x3_full):
    f32 = lambda k: np.asarray(inputs[k], np.float32)
    stag = lambda k: _stage_w(f32(k).T)
    ufT, uiT, ucT = stag("L1_ffn_uf"), stag("L1_ffn_ui"), stag("L1_ffn_uc")
    wfT, wiT, wcT = stag("L1_ffn_wf"), stag("L1_ffn_wi"), stag("L1_ffn_wc")
    woT, uoT = stag("L1_ffn_wo"), stag("L1_ffn_uo")
    in_c = f32("L1_ffn_in_c")
    incc = np.ascontiguousarray(in_c.reshape(8, 128).T)
    incr = in_c.reshape(1, D)
    brows = np.concatenate([f32("L1_ffn_bf"), f32("L1_ffn_bi"),
                            f32("L1_ffn_bc"), f32("L1_ffn_bo")]).reshape(1, 4 * D)
    identf = np.eye(128, dtype=np.float32)
    maps = []
    for cid in range(NCORE):
        maps.append({
            "x3i": np.ascontiguousarray(x3_full[RPC * cid:RPC * (cid + 1), :]),
            "ufT": ufT, "uiT": uiT, "ucT": ucT, "wfT": wfT, "wiT": wiT,
            "wcT": wcT, "woT": woT, "uoT": uoT,
            "incc": incc, "incr": incr, "brows": brows, "identf": identf,
        })
    return maps


def stage_la(inputs, x_full):
    Wq = _stage_w(inputs["L0_attn_Wq"])
    Wk = _stage_w(inputs["L0_attn_Wk"])
    Wv = _stage_w(inputs["L0_attn_Wv"])
    Wo = _stage_w(inputs["L0_attn_Wo"])
    xq = np.ascontiguousarray(x_full[Q0:T, :])
    xctx = np.ascontiguousarray(np.concatenate([x_full[0:4, :], x_full[T - 512:T, :]], axis=0))
    r = np.arange(NQ)[:, None]
    c = np.arange(NC0)[None, :]
    maskadd = np.where(c <= r - 124, 0.0, NEG).astype(np.float32)
    validc = np.ascontiguousarray((r >= 124).astype(np.float32).reshape(5, 128).T)
    identf = np.eye(128, dtype=np.float32)
    maps = []
    for cid in range(NCORE):
        maps.append({
            "x0s": np.ascontiguousarray(x_full[RPC * cid:RPC * (cid + 1), :]),
            "xq": xq, "xctx": xctx,
            "Wq": Wq, "Wk": Wk, "Wv": Wv, "Wo": Wo,
            "maskadd": maskadd, "validc": validc,
            "cmcol": np.full((128, 1), 1.0 if cid == NCORE - 1 else 0.0, np.float32),
            "identf": identf,
        })
    return maps


def kernel(**inputs):
    x_full = np.asarray(inputs["L0_attn_x"], dtype=np.float32).reshape(T, D)
    cores = list(range(NCORE))

    res_a = run_bass_kernel_spmd(build_la(), stage_la(inputs, x_full), core_ids=cores)
    x1 = np.concatenate([res_a.results[c]["x1s"] for c in cores], axis=0)
    mxs = np.concatenate([res_a.results[c]["mx"] for c in cores], axis=1)

    maps_b = stage_lb(inputs, x1)
    for m in maps_b:
        m["mxs"] = np.ascontiguousarray(mxs)
    res_b = run_bass_kernel_spmd(build_lb(), maps_b, core_ids=cores)
    x2 = np.concatenate([res_b.results[c]["x2s"] for c in cores], axis=0)
    cs_all = np.concatenate([res_b.results[c]["cs"] for c in cores], axis=0)

    res_c = run_bass_kernel_spmd(build_lc(),
                                 stage_lc(inputs, x2, np.ascontiguousarray(cs_all), None),
                                 core_ids=cores)
    x3 = np.concatenate([res_c.results[c]["x3s"] for c in cores], axis=0)

    res_d = run_bass_kernel_spmd(build_ld(), stage_ld(inputs, x3), core_ids=cores)
    x4 = np.concatenate([res_d.results[c]["x4s"] for c in cores], axis=0)
    return x4.reshape(1, T, D)


# version 06
# speedup vs baseline: 2.1732x; 1.2186x over previous
from contextlib import ExitStack

import numpy as np
import concourse.bass as bass
import concourse.mybir as mybir
from concourse.bass_utils import run_bass_kernel_spmd

F32 = mybir.dt.float32
BF16 = mybir.dt.bfloat16
AF = mybir.ActivationFunctionType
ALU = mybir.AluOpType
AX = mybir.AxisListType

MAGIC = 12582912.0
NCORE = 8
T = 16384
D = 1024
RPC = T // NCORE
NT = RPC // 128
NQ = 640
NC0 = 516
Q0 = T - NQ
NEG = -30000.0


class Prog:
    def __init__(self, nc, st):
        self.nc = nc
        self.st = st
        self.block = st.enter_context(nc.Block())
        self.sems = {}

    def sem(self, name):
        if name not in self.sems:
            self.sems[name] = [self.st.enter_context(self.nc.semaphore(name)), 0]
        return self.sems[name]

    def inc(self, name, instr, k=1):
        s = self.sem(name)
        instr.then_inc(s[0], k)
        s[1] += k

    def n(self, name):
        return self.sem(name)[1]

    def wait(self, eng, name, val):
        eng.wait_ge(self.sem(name)[0], val)



class DeferredBlock:

    _ENGINES = ("sync", "gpsimd", "tensor", "scalar", "vector")

    def __init__(self, rb):
        self.rb = rb
        self.q = {e: [] for e in self._ENGINES}

    def __getattr__(self, name):
        if name in self._ENGINES:
            return lambda f: self.q[name].append(f)
        raise AttributeError(name)

    def flush(self):
        for e in self._ENGINES:
            fs = self.q[e]
            if not fs:
                continue
            def body(eng, fs=fs):
                for f in fs:
                    f(eng)
            getattr(self.rb, e)(body)


def _stage_w(w):
    return np.ascontiguousarray(
        np.asarray(w, dtype=np.float32).reshape(8, 128, 1024)
        .transpose(1, 0, 2).reshape(128, 8192))



def build_la(upto=6, dbg=False):
    QS = 80
    nc = bass.Bass("TRN2", target_bir_lowering=False)
    x0s = nc.dram_tensor("x0s", [RPC, D], F32, kind="ExternalInput").ap()
    xq = nc.dram_tensor("xq", [QS, D], F32, kind="ExternalInput").ap()
    xctx = nc.dram_tensor("xctx", [NC0, D], F32, kind="ExternalInput").ap()
    Wq = nc.dram_tensor("Wq", [128, 8192], F32, kind="ExternalInput").ap()
    Wk = nc.dram_tensor("Wk", [128, 8192], F32, kind="ExternalInput").ap()
    Wv = nc.dram_tensor("Wv", [128, 8192], F32, kind="ExternalInput").ap()
    Wo = nc.dram_tensor("Wo", [128, 8192], F32, kind="ExternalInput").ap()
    maskadd = nc.dram_tensor("maskadd", [QS, NC0], F32, kind="ExternalInput").ap()
    validc = nc.dram_tensor("validc", [128, 1], F32, kind="ExternalInput").ap()
    tmaskf = nc.dram_tensor("tmaskf", [128, 16], F32, kind="ExternalInput").ap()
    identf = nc.dram_tensor("identf", [128, 128], F32, kind="ExternalInput").ap()
    x1s = nc.dram_tensor("x1s", [RPC, D], F32, kind="ExternalOutput").ap()
    x1q = nc.dram_tensor("x1q", [QS, D], F32, kind="ExternalOutput").ap()
    mx = nc.dram_tensor("mx", [1, 1], F32, kind="ExternalOutput").ap()

    with ExitStack() as st:
        E = st.enter_context
        sXq = E(nc.sbuf_tensor("sXq", [128, D], F32))
        sXQC = E(nc.sbuf_tensor("sXQC", [128, 5 * D], F32))
        sXqT = E(nc.sbuf_tensor("sXqT", [128, 8 * QS], BF16))
        sXcT = E(nc.sbuf_tensor("sXcT", [128, 8 * NC0], BF16))
        sWb = [E(nc.sbuf_tensor(f"sWb{i}", [128, 8192], BF16)) for i in range(2)]
        sQT = E(nc.sbuf_tensor("sQT", [128, 8 * QS], BF16))
        sKT = E(nc.sbuf_tensor("sKT", [128, 8 * NC0], BF16))
        sV = E(nc.sbuf_tensor("sV", [128, 5 * D], BF16))
        sLG = E(nc.sbuf_tensor("sLG", [128, 16 * NC0], F32))
        sMsk = E(nc.sbuf_tensor("sMsk", [128, NC0], F32))
        sVld = E(nc.sbuf_tensor("sVld", [128, 1], F32))
        sTm = E(nc.sbuf_tensor("sTm", [128, 16], F32))
        sId = E(nc.sbuf_tensor("sId", [128, 128], F32))
        sNM = E(nc.sbuf_tensor("sNM", [128, 16], F32))
        sES = E(nc.sbuf_tensor("sES", [128, 16], F32))
        sER = E(nc.sbuf_tensor("sER", [128, 16], F32))
        sWT = E(nc.sbuf_tensor("sWT", [128, 5 * QS], BF16))
        sAT = E(nc.sbuf_tensor("sAT", [128, 8 * QS], BF16))
        sPJ = E(nc.sbuf_tensor("sPJ", [128, D], F32))
        sX0 = E(nc.sbuf_tensor("sX0", [128, 4 * D], F32))
        sSq = E(nc.sbuf_tensor("sSq", [128, 2 * D], F32))
        sX1 = E(nc.sbuf_tensor("sX1", [128, 2 * D], F32))
        sMB = E(nc.sbuf_tensor("sMB", [128, 17], F32))
        sSm = E(nc.sbuf_tensor("sSm", [128, 8], F32))
        sR1 = E(nc.sbuf_tensor("sR1", [1, 128], F32))
        sEps = E(nc.sbuf_tensor("sEps", [128, 1], F32))
        PP = [E(nc.psum_tensor(f"PP{i}", [128, 512], F32)) for i in range(8)]

        p = Prog(nc, st)
        B = DeferredBlock(p.block)

        @B.sync
        def _(sy):
            p.inc("dA", sy.dma_start(out=sXq[0:QS, :], in_=xq[:, :]), 16)
            p.inc("dA", sy.dma_start(out=sVld[:, :], in_=validc[:, :]), 16)
            p.inc("dA", sy.dma_start(out=sTm[:, :], in_=tmaskf[:, :]), 16)
            p.inc("dA", sy.dma_start(out=sId[:, :], in_=identf[:, :]), 16)
            p.inc("dA", sy.dma_start(out=sMsk[0:QS, :], in_=maskadd[:, :]), 16)
            p.inc("dA", sy.dma_start(
                out=sXQC[:, 0:4 * D].rearrange("p (t d) -> p t d", d=D),
                in_=xctx[0:512, :].rearrange("(t p) d -> p t d", p=128)), 16)
            p.inc("dA", sy.dma_start(out=sXQC[0:4, 4 * D:5 * D],
                                     in_=xctx[512:516, :]), 16)

        @B.gpsimd
        def _(g):
            p.inc("dW0", g.dma_start(out=sWb[0][:, :], in_=Wq[:, :]), 16)
            p.inc("dW1", g.dma_start(out=sWb[1][:, :], in_=Wk[:, :]), 16)
            p.wait(g, "Tqk", 8)
            p.inc("dW0", g.dma_start(out=sWb[0][:, :], in_=Wv[:, :]), 16)
            p.wait(g, "Tqk", 24)
            p.inc("dW1", g.dma_start(out=sWb[1][:, :], in_=Wo[:, :]), 16)

        @B.vector
        def _(v):
            p.inc("VV", v.memset(sEps[:, :], 1e-5))
            p.inc("V0", v.memset(sMB[:, :], 0.0))

        @B.tensor
        def _(t):
            p.wait(t, "dA", 16 * 7)
            for i in range(48):
                if i >= 8:
                    p.wait(t, "Str", i - 7)
                if i < 8:
                    src = sXq[0:QS, 128 * i:128 * (i + 1)]
                    out = PP[i % 8][0:128, 0:QS]
                    ident = sId[0:QS, 0:QS]
                elif i < 40:
                    t_, c_ = divmod(i - 8, 8)
                    src = sXQC[:, t_ * D + 128 * c_: t_ * D + 128 * (c_ + 1)]
                    out = PP[i % 8][:, 0:128]
                    ident = sId[:, :]
                else:
                    c_ = i - 40
                    src = sXQC[0:4, 4 * D + 128 * c_: 4 * D + 128 * (c_ + 1)]
                    out = PP[i % 8][0:128, 0:4]
                    ident = sId[0:4, 0:4]
                p.inc("Ttr", t.transpose(out, src, ident))

        @B.scalar
        def _(s):
            for i in range(48):
                p.wait(s, "Ttr", i + 1)
                if i < 8:
                    dst = sXqT[:, QS * i:QS * (i + 1)]
                    ps = PP[i % 8][:, 0:QS]
                elif i < 40:
                    t_, c_ = divmod(i - 8, 8)
                    dst = sXcT[:, c_ * NC0 + 128 * t_: c_ * NC0 + 128 * (t_ + 1)]
                    ps = PP[i % 8][:, 0:128]
                else:
                    c_ = i - 40
                    dst = sXcT[:, c_ * NC0 + 512: c_ * NC0 + 516]
                    ps = PP[i % 8][:, 0:4]
                p.inc("Str", s.copy(dst, ps))

        def _p2_groups():
            out = []
            for oc in range(8):
                out.append((0, oc, 0))
            for oc in range(8):
                for hf in range(2):
                    out.append((1, oc, hf))
            for ct in range(5):
                for hf in range(2):
                    out.append((2, ct, hf))
            return out

        @B.tensor
        def _(t):
            p.wait(t, "Str", 48)
            for g, (ph, a, hf) in enumerate(_p2_groups()):
                if ph == 0 and a == 0:
                    p.wait(t, "dW0", 16)
                if ph == 1 and a == 0 and hf == 0:
                    p.wait(t, "dW1", 16)
                if ph == 2 and a == 0 and hf == 0:
                    p.wait(t, "dW0", 32)
                if g >= 8:
                    p.wait(t, "Sqk", g - 7)
                pb = PP[g % 8]
                wsb = sWb[0] if ph in (0, 2) else sWb[1]
                last = None
                for kc in range(8):
                    if ph == 0:
                        lhsT = wsb[:, kc * 1024 + 128 * a: kc * 1024 + 128 * (a + 1)]
                        rhs = sXqT[:, kc * QS: (kc + 1) * QS]
                        out = pb[:, 0:QS]
                    elif ph == 1:
                        lhsT = wsb[:, kc * 1024 + 128 * a: kc * 1024 + 128 * (a + 1)]
                        rhs = sXcT[:, kc * NC0 + 258 * hf: kc * NC0 + 258 * (hf + 1)]
                        out = pb[:, 0:258]
                    else:
                        rows = 128 if a < 4 else 4
                        lhsT = sXcT[:, kc * NC0 + 128 * a: kc * NC0 + 128 * a + rows]
                        rhs = wsb[:, kc * 1024 + 512 * hf: kc * 1024 + 512 * (hf + 1)]
                        out = pb[0:rows, 0:512]
                    last = t.matmul(out, lhsT, rhs, start=(kc == 0), stop=(kc == 7))
                p.inc("Tqk", last)

        @B.scalar
        def _(s):
            for g, (ph, a, hf) in enumerate(_p2_groups()):
                p.wait(s, "Tqk", g + 1)
                pb = PP[g % 8]
                if ph == 0:
                    dst = sQT[:, a * QS:(a + 1) * QS]
                    src = pb[:, 0:QS]
                elif ph == 1:
                    dst = sKT[:, a * NC0 + 258 * hf: a * NC0 + 258 * (hf + 1)]
                    src = pb[:, 0:258]
                else:
                    rows = 128 if a < 4 else 4
                    dst = sV[0:rows, a * D + 512 * hf: a * D + 512 * (hf + 1)]
                    src = pb[0:rows, 0:512]
                p.inc("Sqk", s.copy(dst, src))

        @B.tensor
        def _(t):
            p.wait(t, "Sqk", 34)
            for pr in range(8):
                if pr >= 2:
                    p.wait(t, "Vlg", pr - 1)
                for sub in range(4):
                    h = 2 * pr + sub // 2
                    ch = sub % 2
                    pb = PP[4 * (pr % 2) + sub]
                    lhsT = sQT[64 * (h % 2):64 * (h % 2) + 64,
                               (h // 2) * QS:(h // 2) * QS + QS]
                    rhs = sKT[64 * (h % 2):64 * (h % 2) + 64,
                              (h // 2) * NC0 + 258 * ch:(h // 2) * NC0 + 258 * (ch + 1)]
                    last = t.matmul(pb[0:QS, 0:258], lhsT, rhs, start=True, stop=True)
                    if sub == 3:
                        p.inc("Tlg", last)

        @B.vector
        def _(v):
            p.wait(v, "dA", 16 * 7)
            for pr in range(8):
                p.wait(v, "Tlg", pr + 1)
                last = None
                for sub in range(4):
                    h = 2 * pr + sub // 2
                    ch = sub % 2
                    pb = PP[4 * (pr % 2) + sub]
                    last = v.scalar_tensor_tensor(
                        sLG[0:QS, h * NC0 + 258 * ch: h * NC0 + 258 * (ch + 1)],
                        pb[0:QS, 0:258], 0.125,
                        sMsk[0:QS, 258 * ch:258 * (ch + 1)],
                        ALU.mult, ALU.add)
                p.inc("Vlg", last)
            p.wait(v, "Vlg", 8)
            p.inc("Vsm", v.tensor_reduce(
                sNM[0:QS, 0:16], sLG[0:QS, :].rearrange("p (h c) -> p h c", c=NC0),
                axis=AX.X, op=ALU.max, negate=True))

        @B.scalar
        def _(s):
            p.wait(s, "Vsm", 1)
            for h in range(16):
                p.inc("Ssm", s.activation(
                    sLG[0:QS, h * NC0:(h + 1) * NC0], sLG[0:QS, h * NC0:(h + 1) * NC0],
                    AF.Exp, bias=sNM[0:QS, h:h + 1]))

        @B.vector
        def _(v):
            p.wait(v, "Ssm", 16)
            p.inc("VS", v.tensor_reduce(
                sES[0:QS, 0:16], sLG[0:QS, :].rearrange("p (h c) -> p h c", c=NC0),
                axis=AX.X, op=ALU.add))
            p.wait(v, "VS", p.n("VS"))
            p.inc("VS", v.reciprocal(sER[0:QS, 0:16], sES[0:QS, 0:16]))
            p.wait(v, "VS", p.n("VS"))
            p.inc("VS", v.tensor_scalar_mul(sER[0:QS, 0:16], sER[0:QS, 0:16],
                                            sVld[0:QS, 0:1]))
            p.wait(v, "VS", p.n("VS"))
            for h in range(16):
                p.inc("Vw", v.tensor_scalar_mul(
                    sLG[0:QS, h * NC0:(h + 1) * NC0],
                    sLG[0:QS, h * NC0:(h + 1) * NC0], sER[0:QS, h:h + 1]))

        for h in range(16):
            @B.tensor
            def _(t, h=h):
                p.wait(t, "Vw", h + 1)
                for cc in range(5):
                    nn = 128 if cc < 4 else 4
                    idx = h * 5 + cc
                    if idx >= 4:
                        p.wait(t, "Swt", idx - 3)
                    src = sLG[0:QS, h * NC0 + 128 * cc: h * NC0 + 128 * cc + nn]
                    p.inc("Twt", t.transpose(PP[3 + idx % 4][0:nn, 0:QS], src,
                                             sId[0:QS, 0:QS]))

            @B.scalar
            def _(s, h=h):
                for cc in range(5):
                    nn = 128 if cc < 4 else 4
                    idx = h * 5 + cc
                    p.wait(s, "Twt", idx + 1)
                    p.inc("Swt", s.copy(sWT[0:nn, QS * cc:QS * (cc + 1)],
                                        PP[3 + idx % 4][0:nn, 0:QS]))

            @B.tensor
            def _(t, h=h):
                p.wait(t, "Swt", h * 5 + 5)
                if h > 0:
                    p.wait(t, "Sav", h)
                last = None
                for cc in range(5):
                    nn = 128 if cc < 4 else 4
                    lhsT = sV[0:nn, cc * D + 64 * h: cc * D + 64 * (h + 1)]
                    rhs = sWT[0:nn, QS * cc:QS * (cc + 1)]
                    last = t.matmul(PP[7][0:64, 0:QS], lhsT, rhs,
                                    start=(cc == 0), stop=(cc == 4))
                p.inc("Tav", last)

            @B.scalar
            def _(s, h=h):
                p.wait(s, "Tav", h + 1)
                p.inc("Sav", s.copy(
                    sAT[64 * (h % 2):64 * (h % 2) + 64,
                        (h // 2) * QS:(h // 2) * QS + QS],
                    PP[7][0:64, 0:QS]))

            @B.scalar
            def _(s, j=h):
                ln_scalar_tile(s, j)

        @B.tensor
        def _(t):
            p.wait(t, "Sav", 16)
            p.wait(t, "dW1", 32)
            for hf in range(2):
                last = None
                for kc in range(8):
                    lhsT = sAT[:, kc * QS:(kc + 1) * QS]
                    rhs = sWb[1][:, kc * 1024 + 512 * hf: kc * 1024 + 512 * (hf + 1)]
                    last = t.matmul(PP[hf][0:QS, 0:512], lhsT, rhs,
                                    start=(kc == 0), stop=(kc == 7))
                p.inc("Tpj", last)

        @B.scalar
        def _(s):
            for hf in range(2):
                p.wait(s, "Tpj", hf + 1)
                p.inc("Spj", s.copy(sPJ[0:QS, 512 * hf:512 * (hf + 1)],
                                    PP[hf][0:QS, 0:512]))

        Vb5 = 1

        def pos_sqsum(j):
            return 7 * j + 2

        def pos_var2(j):
            return 7 * j + 6

        @B.sync
        def _(sy):
            for j in range(NT):
                if j >= 2:
                    p.wait(sy, "Vx1", j - 1)
                    p.inc(f"dO{j % 2}", sy.dma_start(
                        out=x1s[128 * (j - 2):128 * (j - 1), :],
                        in_=sX1[:, ((j - 2) % 2) * D:((j - 2) % 2 + 1) * D]), 16)
                if j >= 4:
                    p.wait(sy, "Vx1", j - 3)
                p.inc(f"dX{j % 4}", sy.dma_start(out=sX0[:, (j % 4) * D:(j % 4 + 1) * D],
                      in_=x0s[128 * j:128 * (j + 1), :]), 16)
            for j in (NT - 2, NT - 1):
                p.wait(sy, "Vx1", j + 1)
                p.inc(f"dO{j % 2}", sy.dma_start(out=x1s[128 * j:128 * (j + 1), :],
                      in_=sX1[:, (j % 2) * D:(j % 2 + 1) * D]), 16)

        def ln_scalar_tile(s, j):
            rows = QS if j == 16 else 128
            if j == 16:
                p.wait(s, "Vq1", 1)
                y = sPJ[0:QS, :]
            else:
                p.wait(s, f"dX{j % 4}", 16 * (j // 4 + 1))
                y = sX0[0:128, (j % 4) * D:(j % 4 + 1) * D]
            if j >= 2:
                p.wait(s, "VV", Vb5 + pos_sqsum(j - 2))
            p.inc("SC", s.activation(sSq[0:rows, (j % 2) * D:(j % 2 + 1) * D],
                                     y, AF.Square))
            p.wait(s, "VV", Vb5 + pos_var2(j))
            p.inc("SC", s.activation(sSm[0:rows, 4:5], sSm[0:rows, 2:3], AF.Ln,
                                     bias=sEps[0:rows, 0:1]))
            p.wait(s, "SC", p.n("SC"))
            p.inc("SC", s.activation(sSm[0:rows, 6:7], sSm[0:rows, 4:5], AF.Exp,
                                     scale=-0.5))

        def ln_vector_tile(v, j):
            rows = QS if j == 16 else 128
            if j == 16:
                p.wait(v, "Spj", 2)
                p.inc("Vq1", v.tensor_add(sPJ[0:QS, :], sPJ[0:QS, :], sXq[0:QS, :]))
                p.wait(v, "Vq1", 1)
                y = sPJ[0:QS, :]
            else:
                p.wait(v, f"dX{j % 4}", 16 * (j // 4 + 1))
                y = sX0[0:128, (j % 4) * D:(j % 4 + 1) * D]
            p.inc("VV", v.reduce_sum(sSm[0:rows, 0:1], y, axis=AX.X))
            p.wait(v, "SC", 3 * j + 1)
            p.inc("VV", v.reduce_sum(sSm[0:rows, 2:3],
                                     sSq[0:rows, (j % 2) * D:(j % 2 + 1) * D],
                                     axis=AX.X))
            p.wait(v, "VV", p.n("VV"))
            p.inc("VV", v.tensor_scalar_mul(sSm[0:rows, 0:1], sSm[0:rows, 0:1],
                                            1.0 / D))
            p.wait(v, "VV", p.n("VV"))
            p.inc("VV", v.tensor_mul(sSm[0:rows, 4:5], sSm[0:rows, 0:1],
                                     sSm[0:rows, 0:1]))
            p.wait(v, "VV", p.n("VV"))
            p.inc("VV", v.tensor_scalar(sSm[0:rows, 2:3], sSm[0:rows, 2:3],
                                        1.0 / D, None, ALU.mult))
            p.wait(v, "VV", p.n("VV"))
            p.inc("VV", v.tensor_sub(sSm[0:rows, 2:3], sSm[0:rows, 2:3],
                                     sSm[0:rows, 4:5]))
            p.wait(v, "SC", 3 * j + 3)
            if j == 16:
                p.inc("VV", v.tensor_scalar(sPJ[0:QS, :], y,
                                            sSm[0:QS, 0:1], sSm[0:QS, 6:7],
                                            ALU.subtract, ALU.mult))
                p.wait(v, "VV", p.n("VV"))
                p.wait(v, "V0", 1)
                p.inc("Vx1", v.reduce_max(sMB[0:QS, 16:17], sPJ[0:QS, :], axis=AX.X,
                                          apply_absolute_value=True))
            else:
                if j >= 2:
                    p.wait(v, f"dO{j % 2}", 16 * (j // 2))
                p.inc("VV", v.tensor_scalar(sX1[:, (j % 2) * D:(j % 2 + 1) * D], y,
                                            sSm[:, 0:1], sSm[:, 6:7],
                                            ALU.subtract, ALU.mult))
                p.wait(v, "VV", p.n("VV"))
                p.wait(v, "V0", 1)
                p.inc("Vx1", v.reduce_max(sMB[:, j:j + 1],
                                          sX1[:, (j % 2) * D:(j % 2 + 1) * D],
                                          axis=AX.X, apply_absolute_value=True))

        @B.vector
        def _(v):
            for j in range(NT):
                ln_vector_tile(v, j)
            ln_vector_tile(v, 16)

        @B.scalar
        def _(s):
            ln_scalar_tile(s, 16)

        @B.vector
        def _(v):
            p.wait(v, "Vx1", 17)
            p.inc("VV", v.tensor_mul(sMB[:, 0:16], sMB[:, 0:16], sTm[:, :]))
            p.wait(v, "VV", p.n("VV"))
            p.inc("Vmx", v.reduce_max(sSm[:, 7:8], sMB[:, 0:17], axis=AX.X))

        @B.tensor
        def _(t):
            p.wait(t, "Vmx", 1)
            p.inc("Tmx", t.transpose(PP[0][0:1, 0:128], sSm[:, 7:8], sId[:, :]))

        @B.scalar
        def _(s):
            p.wait(s, "Tmx", 1)
            p.inc("SCx", s.copy(sR1[0:1, :], PP[0][0:1, 0:128]))

        @B.vector
        def _(v):
            p.wait(v, "SCx", 1)
            p.inc("Vmx", v.reduce_max(sSm[0:1, 0:1], sR1[0:1, :], axis=AX.X))

        @B.sync
        def _(sy):
            p.wait(sy, "Vx1", 17)
            p.inc("dOq", sy.dma_start(out=x1q[:, :], in_=sPJ[0:QS, :]), 16)
            p.wait(sy, "Vmx", 2)
            p.inc("dOm", sy.dma_start(out=mx[:, :], in_=sSm[0:1, 0:1]), 16)
            p.wait(sy, "dO0", 8 * 16)
            p.wait(sy, "dO1", 8 * 16)
            p.wait(sy, "dOq", 16)
            p.wait(sy, "dOm", 16)

        B.flush()
    return nc


def build_lb():
    nc = bass.Bass("TRN2", target_bir_lowering=False)
    x1i = nc.dram_tensor("x1i", [RPC, D], F32, kind="ExternalInput").ap()
    mxs = nc.dram_tensor("mxs", [1, 8], F32, kind="ExternalInput").ap()
    WtT = nc.dram_tensor("WtT", [128, 8192], F32, kind="ExternalInput").ap()
    biasr = nc.dram_tensor("biasr", [1, D], F32, kind="ExternalInput").ap()
    identf = nc.dram_tensor("identf", [128, 128], F32, kind="ExternalInput").ap()
    x2s = nc.dram_tensor("x2s", [RPC, D], F32, kind="ExternalOutput").ap()
    cs = nc.dram_tensor("cs", [1, D], F32, kind="ExternalOutput").ap()

    with ExitStack() as st:
        E = st.enter_context
        sX1 = E(nc.sbuf_tensor("sX1", [128, NT * D], F32))
        sWt = E(nc.sbuf_tensor("sWt", [128, 8192], F32))
        sQW = E(nc.sbuf_tensor("sQW", [128, 8192], BF16))
        sQW2 = E(nc.sbuf_tensor("sQW2", [128, 8192], BF16))
        sQ3 = E(nc.sbuf_tensor("sQ3", [128, 2 * D], F32))
        sQT = E(nc.sbuf_tensor("sQT", [128, 2 * D], BF16))
        sX2 = E(nc.sbuf_tensor("sX2", [128, 2 * D], F32))
        sSq = E(nc.sbuf_tensor("sSq", [128, D], F32))
        sBt = E(nc.sbuf_tensor("sBt", [128, D], F32))
        sMx = E(nc.sbuf_tensor("sMx", [1, 8], F32))
        sBr = E(nc.sbuf_tensor("sBr", [1, D], F32))
        sId = E(nc.sbuf_tensor("sId", [128, 128], F32))
        sOnR = E(nc.sbuf_tensor("sOnR", [1, 128], F32))
        sOnC = E(nc.sbuf_tensor("sOnC", [128, 1], F32))
        sAs = E(nc.sbuf_tensor("sAs", [128, 1], F32))
        sR1 = E(nc.sbuf_tensor("sR1", [1, 128], F32))
        sS1 = E(nc.sbuf_tensor("sS1", [1, 8], F32))
        sCl = E(nc.sbuf_tensor("sCl", [128, 4], F32))
        sSm = E(nc.sbuf_tensor("sSm", [128, 8], F32))
        sCS = E(nc.sbuf_tensor("sCS", [1, D], F32))
        sEps = E(nc.sbuf_tensor("sEps", [128, 1], F32))
        PP = [E(nc.psum_tensor(f"PP{i}", [128, 512], F32)) for i in range(8)]

        p = Prog(nc, st)
        B = DeferredBlock(p.block)

        @B.sync
        def _(sy):
            p.inc("dA", sy.dma_start(out=sX1[:, :].rearrange("p (t d) -> p t d", d=D),
                  in_=x1i.rearrange("(t p) d -> p t d", p=128)), 16)
            p.inc("dA", sy.dma_start(out=sWt[:, :], in_=WtT[:, :]), 16)
            p.inc("dA", sy.dma_start(out=sMx[:, :], in_=mxs[:, :]), 16)
            p.inc("dA", sy.dma_start(out=sBr[:, :], in_=biasr[:, :]), 16)
            p.inc("dA", sy.dma_start(out=sId[:, :], in_=identf[:, :]), 16)

        @B.vector
        def _(v):
            p.inc("VV", v.memset(sEps[:, :], 1e-5))
            p.inc("VV", v.memset(sOnR[:, :], 1.0))
            p.inc("VV", v.memset(sOnC[:, :], 1.0))
            p.wait(v, "dA", 16 * 5)
            p.inc("V1", v.reduce_sum(sAs[:, 0:1], sWt[:, :], axis=AX.X,
                                     apply_absolute_value=True))

        @B.tensor
        def _(t):
            p.wait(t, "V1", 1)
            p.inc("T1", t.transpose(PP[0][0:1, 0:128], sAs[:, 0:1], sId[:, :]))

        @B.scalar
        def _(s):
            p.wait(s, "T1", 1)
            p.inc("S1", s.copy(sR1[0:1, :], PP[0][0:1, 0:128]))

        @B.vector
        def _(v):
            p.wait(v, "S1", 1)
            p.inc("VV", v.reduce_sum(sS1[0:1, 0:1], sR1[0:1, :], axis=AX.X))
            p.wait(v, "VV", p.n("VV"))
            p.inc("VV", v.tensor_scalar_mul(sS1[0:1, 0:1], sS1[0:1, 0:1],
                                            1.0 / (1024.0 * 1024.0)))
            p.wait(v, "VV", p.n("VV"))
            p.inc("VV", v.tensor_scalar_mul(sS1[0:1, 1:2], sS1[0:1, 0:1], 0.5))
            p.inc("VV", v.reduce_max(sS1[0:1, 2:3], sMx[0:1, :], axis=AX.X))
            p.wait(v, "VV", p.n("VV"))
            p.inc("VV", v.tensor_scalar_mul(sS1[0:1, 3:4], sS1[0:1, 2:3], 1.0 / 127.0))
            p.wait(v, "VV", p.n("VV"))
            p.inc("VV", v.reciprocal(sS1[0:1, 4:5], sS1[0:1, 3:4]))
            p.wait(v, "VV", p.n("VV"))
            p.inc("V1", v.tensor_mul(sS1[0:1, 5:6], sS1[0:1, 0:1], sS1[0:1, 3:4]))
            p.wait(v, "V1", p.n("V1"))

        @B.tensor
        def _(t):
            p.wait(t, "V1", 2)
            p.inc("T1", t.matmul(PP[1][:, 0:1], sOnR[0:1, :], sS1[0:1, 4:5],
                                 start=True, stop=True))
            p.inc("T1", t.matmul(PP[2][:, 0:1], sOnR[0:1, :], sS1[0:1, 1:2],
                                 start=True, stop=True))
            p.inc("T1", t.matmul(PP[3][:, 0:1], sOnR[0:1, :], sS1[0:1, 5:6],
                                 start=True, stop=True))
            p.inc("T1", t.matmul(PP[4][:, 0:512], sOnR[0:1, :], sBr[0:1, 0:512],
                                 start=True, stop=True))
            p.inc("T1", t.matmul(PP[5][:, 0:512], sOnR[0:1, :], sBr[0:1, 512:1024],
                                 start=True, stop=True))

        @B.scalar
        def _(s):
            p.wait(s, "T1", 6)
            p.inc("S1", s.copy(sCl[:, 0:1], PP[1][:, 0:1]))
            p.inc("S1", s.copy(sCl[:, 1:2], PP[2][:, 0:1]))
            p.inc("S1", s.copy(sCl[:, 2:3], PP[3][:, 0:1]))
            p.inc("S1", s.copy(sBt[:, 0:512], PP[4][:, 0:512]))
            p.inc("S1", s.copy(sBt[:, 512:1024], PP[5][:, 0:512]))

        @B.vector
        def _(v):
            p.wait(v, "S1", 6)
            p.inc("VV", v.tensor_scalar_mul(sCl[:, 3:4], sCl[:, 1:2], -1.0))
            p.wait(v, "VV", p.n("VV"))
            p.inc("VV", v.tensor_scalar(sQW[:, :], sWt[:, :], sCl[:, 1:2], None,
                                        ALU.is_gt))
            p.inc("VV", v.tensor_scalar(sQW2[:, :], sWt[:, :], sCl[:, 3:4], None,
                                        ALU.is_lt))
            p.wait(v, "VV", p.n("VV"))
            p.inc("Vqw", v.tensor_sub(sQW[:, :], sQW[:, :], sQW2[:, :]))

        def emit_quant(v, j):
            pa = j % 2
            p.inc("VV", v.tensor_scalar(sQ3[:, pa * D:(pa + 1) * D],
                                        sX1[:, j * D:(j + 1) * D],
                                        sCl[:, 0:1], MAGIC, ALU.mult, ALU.add))
            p.wait(v, "VV", p.n("VV"))
            p.inc("VV", v.tensor_scalar(sQ3[:, pa * D:(pa + 1) * D],
                                        sQ3[:, pa * D:(pa + 1) * D],
                                        -MAGIC, -128.0, ALU.add, ALU.max))
            p.wait(v, "VV", p.n("VV"))
            p.inc("Vq", v.tensor_scalar(sQ3[:, pa * D:(pa + 1) * D],
                                        sQ3[:, pa * D:(pa + 1) * D],
                                        127.0, None, ALU.min))

        def emit_post(v, j):
            pa = j % 2
            p.wait(v, "Tmm", j + 1)
            p.inc("VV", v.scalar_tensor_tensor(sQ3[:, pa * D: pa * D + 512],
                                               PP[4][:, 0:512], sCl[:, 2:3],
                                               sBt[:, 0:512], ALU.mult, ALU.add))
            p.inc("VV", v.scalar_tensor_tensor(sQ3[:, pa * D + 512: pa * D + 1024],
                                               PP[5][:, 0:512], sCl[:, 2:3],
                                               sBt[:, 512:1024], ALU.mult, ALU.add))
            p.wait(v, "VV", p.n("VV"))
            p.inc("Vy", v.tensor_add(sQ3[:, pa * D:(pa + 1) * D],
                                     sQ3[:, pa * D:(pa + 1) * D],
                                     sX1[:, j * D:(j + 1) * D]))
            p.wait(v, "Vy", j + 1)
            p.inc("VV", v.reduce_sum(sSm[:, pa:pa + 1],
                                     sQ3[:, pa * D:(pa + 1) * D], axis=AX.X))
            p.wait(v, "SC5", 3 * j + 1)
            p.inc("Vsq", v.reduce_sum(sSm[:, 2 + pa:3 + pa], sSq[:, :], axis=AX.X))
            p.wait(v, "Vsq", j + 1)
            p.inc("VV", v.tensor_scalar_mul(sSm[:, pa:pa + 1], sSm[:, pa:pa + 1],
                                            1.0 / D))
            p.wait(v, "VV", p.n("VV"))
            p.inc("VV", v.tensor_mul(sSm[:, 4 + pa:5 + pa], sSm[:, pa:pa + 1],
                                     sSm[:, pa:pa + 1]))
            p.wait(v, "VV", p.n("VV"))
            p.inc("VV", v.tensor_scalar(sSm[:, 2 + pa:3 + pa], sSm[:, 2 + pa:3 + pa],
                                        1.0 / D, None, ALU.mult))
            p.wait(v, "VV", p.n("VV"))
            p.inc("Vmu", v.tensor_sub(sSm[:, 2 + pa:3 + pa], sSm[:, 2 + pa:3 + pa],
                                      sSm[:, 4 + pa:5 + pa]))
            p.wait(v, "SC5", 3 * j + 3)
            if j >= 2:
                p.wait(v, f"dO{j % 2}", 16 * (j // 2))
                p.wait(v, "Tcs", j - 1)
            p.inc("Vx2", v.tensor_scalar(sX2[:, pa * D:(pa + 1) * D],
                                         sQ3[:, pa * D:(pa + 1) * D],
                                         sSm[:, pa:pa + 1], sSm[:, 6 + pa:7 + pa],
                                         ALU.subtract, ALU.mult))

        @B.vector
        def _(v):
            p.wait(v, "S1", 6)
            emit_quant(v, 0)
            emit_quant(v, 1)
            for j in range(NT):
                emit_post(v, j)
                if j + 2 < NT:
                    p.wait(v, "Vx2", j + 1)
                    emit_quant(v, j + 2)

        @B.tensor
        def _(t):
            p.wait(t, "Vqw", 1)
            p.wait(t, "S1", 6)
            for j in range(NT):
                pa = j % 2
                p.wait(t, "Vq", j + 1)
                for c in range(8):
                    idx = 8 * j + c
                    if idx >= 4:
                        p.wait(t, "Sqt", idx - 3)
                    p.inc("Ttr", t.transpose(PP[idx % 4][:, 0:128],
                                             sQ3[:, pa * D + 128 * c: pa * D + 128 * (c + 1)],
                                             sId[:, :]))
                p.wait(t, "Sqt", 8 * (j + 1))
                if j >= 1:
                    p.wait(t, "Vy", j)
                for hf in range(2):
                    last = None
                    for kc in range(8):
                        last = t.matmul(PP[4 + hf][:, 0:512],
                                        sQT[:, pa * D + 128 * kc: pa * D + 128 * (kc + 1)],
                                        sQW[:, kc * 1024 + 512 * hf: kc * 1024 + 512 * (hf + 1)],
                                        start=(kc == 0), stop=(kc == 7))
                p.inc("Tmm", last)
                if j >= 1:
                    p.wait(t, "Vx2", j)
                    t.matmul(PP[6][0:1, 0:512], sOnC[:, 0:1],
                             sX2[:, ((j - 1) % 2) * D:((j - 1) % 2) * D + 512],
                             start=(j == 1), stop=False)
                    last = t.matmul(PP[7][0:1, 0:512], sOnC[:, 0:1],
                                    sX2[:, ((j - 1) % 2) * D + 512:((j - 1) % 2) * D + 1024],
                                    start=(j == 1), stop=False)
                    p.inc("Tcs", last)
            p.wait(t, "Vx2", NT)
            t.matmul(PP[6][0:1, 0:512], sOnC[:, 0:1],
                     sX2[:, ((NT - 1) % 2) * D:((NT - 1) % 2) * D + 512],
                     start=False, stop=True)
            last = t.matmul(PP[7][0:1, 0:512], sOnC[:, 0:1],
                            sX2[:, ((NT - 1) % 2) * D + 512:((NT - 1) % 2) * D + 1024],
                            start=False, stop=True)
            p.inc("Tcs", last)

        @B.scalar
        def _(s):
            for j in range(NT):
                pa = j % 2
                for c in range(8):
                    idx = 8 * j + c
                    p.wait(s, "Ttr", idx + 1)
                    p.inc("Sqt", s.copy(sQT[:, pa * D + 128 * c: pa * D + 128 * (c + 1)],
                                        PP[idx % 4][:, 0:128]))
                p.wait(s, "Vy", j + 1)
                if j >= 1:
                    p.wait(s, "Vsq", j)
                p.inc("SC5", s.activation(sSq[:, :], sQ3[:, pa * D:(pa + 1) * D],
                                          AF.Square))
                p.wait(s, "Vmu", j + 1)
                p.inc("SC5", s.activation(sSm[:, 4 + pa:5 + pa], sSm[:, 2 + pa:3 + pa],
                                          AF.Ln, bias=sEps[:, 0:1]))
                p.wait(s, "SC5", p.n("SC5"))
                p.inc("SC5", s.activation(sSm[:, 6 + pa:7 + pa], sSm[:, 4 + pa:5 + pa],
                                          AF.Exp, scale=-0.5))

        @B.sync
        def _(sy):
            for j in range(NT):
                p.wait(sy, "Vx2", j + 1)
                p.inc(f"dO{j % 2}", sy.dma_start(out=x2s[128 * j:128 * (j + 1), :],
                      in_=sX2[:, (j % 2) * D:(j % 2 + 1) * D]), 16)

        @B.scalar
        def _(s):
            p.wait(s, "Tcs", NT)
            p.inc("Scs", s.copy(sCS[0:1, 0:512], PP[6][0:1, 0:512]))
            p.inc("Scs", s.copy(sCS[0:1, 512:1024], PP[7][0:1, 0:512]))

        @B.sync
        def _(sy):
            p.wait(sy, "Scs", 2)
            p.inc("dOc", sy.dma_start(out=cs[:, :], in_=sCS[0:1, :]), 16)
            p.wait(sy, "dO0", 8 * 16)
            p.wait(sy, "dO1", 8 * 16)
            p.wait(sy, "dOc", 16)

        B.flush()
    return nc


def stage_lb(inputs, x1_full):
    WtT = _stage_w(np.asarray(inputs["L0_ffn_weight"], np.float32).T)
    biasr = np.asarray(inputs["L0_ffn_bias"], np.float32).reshape(1, D)
    identf = np.eye(128, dtype=np.float32)
    maps = []
    for cid in range(NCORE):
        maps.append({
            "x1i": np.ascontiguousarray(x1_full[RPC * cid:RPC * (cid + 1), :]),
            "mxs": None,
            "WtT": WtT, "biasr": biasr, "identf": identf,
        })
    return maps



def build_lc():
    nc = bass.Bass("TRN2", target_bir_lowering=False)
    x2i = nc.dram_tensor("x2i", [RPC, D], F32, kind="ExternalInput").ap()
    x2h = nc.dram_tensor("x2h", [32, D], F32, kind="ExternalInput").ap()
    csall = nc.dram_tensor("csall", [8, D], F32, kind="ExternalInput").ap()
    prefm = nc.dram_tensor("prefm", [8, 1], F32, kind="ExternalInput").ap()
    Wdb = nc.dram_tensor("Wdb", [128, 62], F32, kind="ExternalInput").ap()
    Wgb = nc.dram_tensor("Wgb", [128, 62], F32, kind="ExternalInput").ap()
    bped = nc.dram_tensor("bped", [1, 496], F32, kind="ExternalInput").ap()
    bpeg = nc.dram_tensor("bpeg", [1, 496], F32, kind="ExternalInput").ap()
    sttr = nc.dram_tensor("sttr", [1, 496], F32, kind="ExternalInput").ap()
    rsel = nc.dram_tensor("rsel", [16, 496], F32, kind="ExternalInput").ap()
    WfcT = nc.dram_tensor("WfcT", [128, 8192], F32, kind="ExternalInput").ap()
    gmr = nc.dram_tensor("gmr", [1, D], F32, kind="ExternalInput").ap()
    btr = nc.dram_tensor("btr", [1, D], F32, kind="ExternalInput").ap()
    recipc = nc.dram_tensor("recipc", [128, 16], F32, kind="ExternalInput").ap()
    reciph = nc.dram_tensor("reciph", [32, 1], F32, kind="ExternalInput").ap()
    trif = nc.dram_tensor("trif", [128, 128], F32, kind="ExternalInput").ap()
    nsl = nc.dram_tensor("nsl", [32, 32], F32, kind="ExternalInput").ap()
    cmsk = nc.dram_tensor("cmsk", [128, 496], F32, kind="ExternalInput").ap()
    identf = nc.dram_tensor("identf", [128, 128], F32, kind="ExternalInput").ap()
    x3s = nc.dram_tensor("x3s", [RPC, D], F32, kind="ExternalOutput").ap()
    SLOT = 16 * 129 * 192
    wz = nc.dram_tensor("wz", [NT * SLOT], BF16)

    with ExitStack() as st:
        E = st.enter_context
        sX2b = E(nc.sbuf_tensor("sX2b", [128, NT * D], BF16))
        sX2r = E(nc.sbuf_tensor("sX2r", [128, 2 * D], F32))
        sX2h = E(nc.sbuf_tensor("sX2h", [32, D], F32))
        sCsA = E(nc.sbuf_tensor("sCsA", [8, D], F32))
        sCsM = sCsA
        sPfm = E(nc.sbuf_tensor("sPfm", [8, 1], F32))
        sRun = E(nc.sbuf_tensor("sRun", [1, D], F32))
        sHci = E(nc.sbuf_tensor("sHci", [32, D], BF16))
        sCiH = E(nc.sbuf_tensor("sCiH", [64, D], BF16))
        sCiF = E(nc.sbuf_tensor("sCiF", [128, 2 * D], F32))
        sCi = E(nc.sbuf_tensor("sCi", [128, NT * D], BF16))
        sCiLo = E(nc.sbuf_tensor("sCiLo", [64, 2 * D], BF16))
        sCiT = E(nc.sbuf_tensor("sCiT", [128, 8 * 128], BF16))
        sWdb = E(nc.sbuf_tensor("sWdb", [128, 62], BF16))
        sWgb = E(nc.sbuf_tensor("sWgb", [128, 62], BF16))
        sBpdT = E(nc.sbuf_tensor("sBpdT", [128, 496], F32))
        sBpgT = E(nc.sbuf_tensor("sBpgT", [128, 496], F32))
        sSttT = E(nc.sbuf_tensor("sSttT", [128, 496], F32))
        sG = E(nc.sbuf_tensor("sG", [128, 496], F32))
        sDy = E(nc.sbuf_tensor("sDy", [128, 496], F32))
        sEn = sDy
        sEs = E(nc.sbuf_tensor("sEs", [128, 16], F32))
        sErc = E(nc.sbuf_tensor("sErc", [128, 16], F32))
        sErT = E(nc.sbuf_tensor("sErT", [16, 128], F32))
        sRs = E(nc.sbuf_tensor("sRs", [16, 496], F32))
        sWall = E(nc.sbuf_tensor("sWall", [128, 496], F32))
        sCmsk = E(nc.sbuf_tensor("sCmsk", [128, 496], F32))
        sPad = E(nc.sbuf_tensor("sPad", [128, 2 * 16 * 192], BF16))
        sTab = E(nc.sbuf_tensor("sTab", [128, 2 * 256], BF16))
        sCLT = E(nc.sbuf_tensor("sCLT", [128, 2 * D], BF16))
        sWfc = E(nc.sbuf_tensor("sWfc", [128, 8192], BF16))
        sGmT = E(nc.sbuf_tensor("sGmT", [128, D], F32))
        sBtT = E(nc.sbuf_tensor("sBtT", [128, D], F32))
        sYv = E(nc.sbuf_tensor("sYv", [128, D], F32))
        sXln = E(nc.sbuf_tensor("sXln", [128, D], F32))
        sSq = E(nc.sbuf_tensor("sSq", [128, D], F32))
        sX3 = E(nc.sbuf_tensor("sX3", [128, 2 * D], F32))
        sRc = E(nc.sbuf_tensor("sRc", [128, 16], F32))
        sRh = E(nc.sbuf_tensor("sRh", [32, 1], F32))
        sTrib = E(nc.sbuf_tensor("sTrib", [128, 128], BF16))
        sNsl = E(nc.sbuf_tensor("sNsl", [32, 32], F32))
        sId = E(nc.sbuf_tensor("sId", [128, 128], F32))
        sIdB = E(nc.sbuf_tensor("sIdB", [128, 128], BF16))
        sOnR = E(nc.sbuf_tensor("sOnR", [1, 128], F32))
        sOnC = E(nc.sbuf_tensor("sOnC", [128, 1], F32))
        sOnCb = E(nc.sbuf_tensor("sOnCb", [128, 1], BF16))
        sZro = E(nc.sbuf_tensor("sZro", [128, 1032], BF16))
        sSm = E(nc.sbuf_tensor("sSm", [128, 8], F32))
        sEp6 = E(nc.sbuf_tensor("sEp6", [128, 1], F32))
        sEp5 = E(nc.sbuf_tensor("sEp5", [128, 1], F32))
        PF = [E(nc.psum_tensor(f"PF{i}", [128, 512], F32)) for i in range(6)]
        PB = [E(nc.psum_tensor(f"PB{i}", [128, 1024], BF16)) for i in range(2)]

        p = Prog(nc, st)
        B = DeferredBlock(p.block)

        @B.sync
        def _(sy):
            for dst, srcx in ((sX2h, x2h), (sCsA, csall), (sPfm, prefm),
                              (sBpdT[0:1, :], bped), (sBpgT[0:1, :], bpeg),
                              (sSttT[0:1, :], sttr), (sRs, rsel),
                              (sGmT[0:1, :], gmr), (sBtT[0:1, :], btr),
                              (sRc, recipc), (sRh, reciph),
                              (sNsl, nsl), (sCmsk, cmsk), (sId, identf)):
                p.inc("dA", sy.dma_start(out=dst[:, :] if dst.shape[0] != 1 else dst,
                                         in_=srcx[:, :]), 16)

        @B.gpsimd
        def _(g):
            p.inc("dW", g.dma_start(out=sWdb[:, :], in_=Wdb[:, :]), 16)
            p.inc("dW", g.dma_start(out=sWgb[:, :], in_=Wgb[:, :]), 16)
            p.inc("dW", g.dma_start(out=sWfc[:, :], in_=WfcT[:, :]), 16)
            p.inc("dW", g.dma_start(out=sIdB[:, :], in_=identf[:, :]), 16)
            p.inc("dW", g.dma_start(out=sTrib[:, :], in_=trif[:, :]), 16)
            p.inc("dXb", g.dma_start(
                out=sX2b[:, :].rearrange("p (t d) -> p t d", d=D),
                in_=x2i.rearrange("(t p) d -> p t d", p=128)), 16)

        @B.vector
        def _(v):
            p.inc("VV", v.memset(sEp6[:, :], 1e-6))
            p.inc("VV", v.memset(sEp5[:, :], 1e-5))
            p.inc("VV", v.memset(sOnR[:, :], 1.0))
            p.inc("VV", v.memset(sOnC[:, :], 1.0))
            p.inc("VV", v.memset(sOnCb[:, :], 1.0))
            p.inc("V0", v.memset(sZro[:, :], 0.0))

        @B.gpsimd
        def _(g):
            p.inc("G1", g.memset(sCiH[:, :], 0.0))

        @B.tensor
        def _(t):
            p.wait(t, "dA", 14 * 16)
            p.wait(t, "VV", 5)
            p.inc("T0", t.matmul(PF[0][:, 0:496], sOnR[0:1, :], sBpdT[0:1, :],
                                 start=True, stop=True))
            p.inc("T0", t.matmul(PF[1][:, 0:496], sOnR[0:1, :], sBpgT[0:1, :],
                                 start=True, stop=True))
            p.inc("T0", t.matmul(PF[2][:, 0:496], sOnR[0:1, :], sSttT[0:1, :],
                                 start=True, stop=True))
            p.inc("T0", t.matmul(PF[3][:, 0:512], sOnR[0:1, :], sGmT[0:1, 0:512],
                                 start=True, stop=True))
            p.inc("T0", t.matmul(PF[4][:, 0:512], sOnR[0:1, :], sGmT[0:1, 512:1024],
                                 start=True, stop=True))
            p.inc("T0", t.matmul(PF[5][:, 0:512], sOnR[0:1, :], sBtT[0:1, 0:512],
                                 start=True, stop=True))

        @B.scalar
        def _(s):
            p.wait(s, "T0", 6)
            p.inc("S0", s.copy(sBpdT[:, :], PF[0][:, 0:496]))
            p.inc("S0", s.copy(sBpgT[:, :], PF[1][:, 0:496]))
            p.inc("S0", s.copy(sSttT[:, :], PF[2][:, 0:496]))
            p.inc("S0", s.copy(sGmT[:, 0:512], PF[3][:, 0:512]))
            p.inc("S0", s.copy(sGmT[:, 512:1024], PF[4][:, 0:512]))
            p.inc("S0", s.copy(sBtT[:, 0:512], PF[5][:, 0:512]))

        @B.tensor
        def _(t):
            p.wait(t, "S0", 6)
            p.inc("T0", t.matmul(PF[0][:, 0:512], sOnR[0:1, :], sBtT[0:1, 512:1024],
                                 start=True, stop=True))

        @B.scalar
        def _(s):
            p.wait(s, "T0", 7)
            p.inc("S0", s.copy(sBtT[:, 512:1024], PF[0][:, 0:512]))

        @B.vector
        def _(v):
            p.wait(v, "dA", 14 * 16)
            p.inc("V1", v.tensor_scalar_mul(sCsA[0:8, :], sCsA[0:8, :], sPfm[0:8, 0:1]))

        @B.tensor
        def _(t):
            p.wait(t, "V1", 1)
            p.wait(t, "S0", 7)
            for hf in range(2):
                p.inc("T1", t.matmul(PF[hf][0:1, 0:512], sOnC[0:8, 0:1],
                                     sCsM[0:8, 512 * hf:512 * (hf + 1)],
                                     start=True, stop=True))

        @B.scalar
        def _(s):
            p.wait(s, "T1", 2)
            for hf in range(2):
                p.inc("Sof", s.copy(sRun[0:1, 512 * hf:512 * (hf + 1)],
                                    PF[hf][0:1, 0:512]))

        @B.tensor
        def _(t):
            p.wait(t, "Sof", 2)
            for hf in range(2):
                t.matmul(PF[hf][0:32, 0:512], sNsl[0:32, :],
                         sX2h[0:32, 512 * hf:512 * (hf + 1)], start=True, stop=False)
                p.inc("Thl", t.matmul(PF[hf][0:32, 0:512], sOnR[0:1, 0:32],
                                      sRun[0:1, 512 * hf:512 * (hf + 1)],
                                      start=False, stop=True))

        @B.scalar
        def _(s):
            p.wait(s, "Thl", 2)
            for hf in range(2):
                p.inc("Shl", s.activation(sHci[0:32, 512 * hf:512 * (hf + 1)],
                                          PF[hf][0:32, 0:512], AF.Copy,
                                          scale=sRh[0:32, 0:1]))

        @B.gpsimd
        def _(g):
            p.wait(g, "Shl", 2)
            p.wait(g, "G1", 1)
            p.inc("dHl", g.dma_start(out=sCiH[32:64, :], in_=sHci[0:32, :]), 16)

        def emit_erec(t, jj):
            it = min(jj + 1, NT - 1)
            p.wait(t, "Ves", jj + 1)
            p.wait(t, "Sct", 8 * it + 5)
            p.inc("Tet", t.transpose(PF[0][0:16, 0:128], sErc[:, 0:16], sId[:, :]))
            p.wait(t, "Set", jj + 1)
            p.wait(t, "Sct", 8 * it + 6)
            if jj >= 1:
                p.wait(t, "Vw", jj + 1)
            p.inc("Ter", t.matmul(PF[1][:, 0:496], sErT[0:16, :], sRs[0:16, :],
                                  start=True, stop=True))

        @B.tensor
        def _(t):
            for j in range(NT):
                pa = j % 2
                p.wait(t, "Vrun", j)
                if j == 0:
                    p.wait(t, "Shl", 2)
                    p.wait(t, "dXb", 16)
                    p.wait(t, "dW", 80)
                elif j == 1:
                    p.wait(t, "Sct", 8 * (j - 1) + 6)
                else:
                    p.wait(t, "Set", j - 1)
                    p.wait(t, "Vw", j)
                for hf in range(2):
                    t.matmul(PF[hf][:, 0:512], sTrib[:, :],
                             sX2b[:, j * D + 512 * hf: j * D + 512 * (hf + 1)],
                             start=True, stop=False)
                    p.inc("Tcu", t.matmul(PF[hf][:, 0:512], sOnR[0:1, :],
                                          sRun[0:1, 512 * hf:512 * (hf + 1)],
                                          start=False, stop=True))
                for hf in range(2):
                    p.inc("Tco", t.matmul(PF[2 + hf][0:1, 0:512], sOnCb[:, 0:1],
                                          sX2b[:, j * D + 512 * hf: j * D + 512 * (hf + 1)],
                                          start=True, stop=True))
                p.wait(t, "Scu", 2 * (j + 1))
                if j >= 1:
                    p.wait(t, "Vc4", j)
                for c in range(8):
                    idx = 8 * j + c
                    if idx >= 4:
                        p.wait(t, "Sct", idx - 3)
                    bank = (0, 1, 4, 5)[c % 4]
                    p.inc("Ttr", t.transpose(PF[bank][:, 0:128],
                                             sCiF[:, pa * D + 128 * c: pa * D + 128 * (c + 1)],
                                             sId[:, :]))
                if j >= 1:
                    emit_erec(t, j - 1)
                p.wait(t, "Sct", 8 * (j + 1))
                if j == 0:
                    p.wait(t, "dW", 80)
                last_g = None
                for kc in range(8):
                    t.matmul(PF[4][:, 62 * kc:62 * (kc + 1)],
                             sCiT[:, 128 * kc:128 * (kc + 1)], sWdb[:, :],
                             start=(kc == 0), stop=(kc == 7))
                    last_g = t.matmul(PF[5][:, 62 * kc:62 * (kc + 1)],
                                      sCiT[:, 128 * kc:128 * (kc + 1)], sWgb[:, :],
                                      start=(kc == 0), stop=(kc == 7))
                p.inc("Tpc", last_g)
            emit_erec(t, NT - 1)

        @B.scalar
        def _(s):
            for j in range(NT):
                pa = j % 2
                p.wait(s, "Tcu", 2 * (j + 1))
                for hf in range(2):
                    p.inc("Scu", s.activation(
                        sCiF[:, pa * D + 512 * hf: pa * D + 512 * (hf + 1)],
                        PF[hf][:, 0:512], AF.Copy, scale=sRc[:, j:j + 1]))
                p.wait(s, "Scu", p.n("Scu"))
                p.inc("Sci", s.copy(sCi[:, j * D:(j + 1) * D],
                                    sCiF[:, pa * D:(pa + 1) * D]))
                for c in range(8):
                    idx = 8 * j + c
                    p.wait(s, "Ttr", idx + 1)
                    p.inc("Sct", s.copy(sCiT[:, 128 * c:128 * (c + 1)],
                                        PF[(0, 1, 4, 5)[c % 4]][:, 0:128]))
                if j >= 1:
                    p.wait(s, "Tet", j)
                    p.inc("Set", s.copy(sErT[0:16, :], PF[0][0:16, 0:128]))
                p.wait(s, "Vg", j + 1)
                p.inc("Sex", s.activation(sG[:, :], sG[:, :], AF.Exp, scale=-1.0))
                p.wait(s, "Sex", p.n("Sex"))
                p.inc("Sex", s.activation(sG[:, :], sG[:, :], AF.Ln,
                                          bias=sOnC[:, 0:1]))
                p.wait(s, "Sex", p.n("Sex"))
                p.inc("Sex", s.activation(sG[:, :], sG[:, :], AF.Exp, scale=-1.0))
                p.wait(s, "Ven", j + 1)
                p.inc("Sen", s.activation(sDy[:, :], sDy[:, :], AF.Exp))
            p.wait(s, "Tet", NT)
            p.inc("Set", s.copy(sErT[0:16, :], PF[0][0:16, 0:128]))

        @B.vector
        def _(v):
            for j in range(NT):
                p.wait(v, "Tco", 2 * (j + 1))
                v.tensor_add(sRun[0:1, 0:512], sRun[0:1, 0:512], PF[2][0:1, 0:512])
                p.inc("Vrun", v.tensor_add(sRun[0:1, 512:1024], sRun[0:1, 512:1024],
                                           PF[3][0:1, 0:512]))
                p.wait(v, "Tpc", j + 1)
                p.inc("Vg", v.tensor_add(sG[:, :], PF[5][:, 0:496], sBpgT[:, :]))
                p.inc("Vc4", v.tensor_add(sDy[:, :], PF[4][:, 0:496], sBpdT[:, :]))
                p.wait(v, "Vc4", j + 1)
                p.wait(v, "Sex", 3 * (j + 1))
                p.inc("VV", v.tensor_mul(sDy[:, :], sG[:, :], sDy[:, :]))
                p.wait(v, "VV", p.n("VV"))
                p.inc("Ven", v.tensor_add(sDy[:, :], sDy[:, :], sSttT[:, :]))
                p.wait(v, "Sen", j + 1)
                p.inc("VV", v.tensor_reduce(sEs[:, 0:16],
                      sDy[:, :].rearrange("p (h k) -> p h k", k=31),
                      axis=AX.X, op=ALU.add))
                p.wait(v, "VV", p.n("VV"))
                p.inc("Ves", v.reciprocal(sErc[:, 0:16], sEs[:, 0:16]))
                p.wait(v, "Ter", j + 1)
                if j >= 1:
                    p.wait(v, f"dSt{(j - 1) % 2}", 16 * ((j - 1) // 2 + 1))
                p.inc("Vw", v.tensor_mul(sWall[:, :], sDy[:, :], PF[1][:, 0:496]))
                if j == 0:
                    p.wait(v, "Vw", p.n("Vw"))
                    p.inc("Vw", v.tensor_mul(sWall[:, :], sWall[:, :], sCmsk[:, :]))
                p.wait(v, "Vw", p.n("Vw"))

        @B.gpsimd
        def _(g):
            p.wait(g, "V0", 1)
            for sl in range(16):
                for th in range(3):
                    p.inc("dZ", g.dma_start(
                        out=bass.AP(wz, sl * SLOT + th * 128 * 1032,
                                    [[1032, 128], [1, 1032]]),
                        in_=sZro[:, :]), 16)
            for j in range(NT):
                p.wait(g, "Vw", j + 2)
                if j == 0:
                    p.wait(g, "dZ", 16 * 48)
                p.inc(f"dSt{j % 2}", g.dma_start(
                    out=bass.AP(wz, j * SLOT + 192,
                                [[192, 128], [129 * 192, 16], [1, 31]]),
                    in_=sWall[:, :].rearrange("p (h k) -> p h k", k=31)), 16)
            for j in range(NT - 1):
                pa = j % 2
                if j >= 2:
                    p.wait(g, "Tbd", 16 * j)
                p.inc(f"dCl{pa}", g.dma_start(out=sCiLo[0:64, pa * D:(pa + 1) * D],
                      in_=sCi[64:128, j * D:(j + 1) * D]), 16)

        @B.sync
        def _(sy):
            for j in range(NT):
                p.wait(sy, f"dSt{j % 2}", 16 * (j // 2 + 1))
                if j >= 2:
                    p.wait(sy, "Ttb", 16 * (j - 1))
                p.inc(f"dSk{j % 2}", sy.dma_start(
                    out=sPad[:, (j % 2) * 3072:(j % 2 + 1) * 3072]
                        .rearrange("p (h c) -> p h c", c=192),
                    in_=bass.AP(wz, j * SLOT + 192 - 34,
                                [[191, 128], [129 * 192, 16], [1, 192]])), 16)
                if j >= 2:
                    p.wait(sy, "Vy2", j - 1)
                p.inc(f"dXr{j % 2}", sy.dma_start(
                    out=sX2r[:, (j % 2) * D:(j % 2 + 1) * D],
                    in_=x2i[128 * j:128 * (j + 1), :]), 16)
                if j >= 2:
                    p.wait(sy, "Vx3", j - 1)
                    p.inc(f"dO{j % 2}", sy.dma_start(
                        out=x3s[128 * (j - 2):128 * (j - 1), :],
                        in_=sX3[:, ((j - 2) % 2) * D:((j - 2) % 2 + 1) * D]), 16)
            for j in (NT - 2, NT - 1):
                p.wait(sy, "Vx3", j + 1)
                p.inc(f"dO{j % 2}", sy.dma_start(out=x3s[128 * j:128 * (j + 1), :],
                      in_=sX3[:, (j % 2) * D:(j % 2 + 1) * D]), 16)
            p.wait(sy, "dO0", 8 * 16)
            p.wait(sy, "dO1", 8 * 16)

        @B.tensor
        def _(t):
            for hi in range(NT * 16 + 2):
                if hi < NT * 16:
                    j, h = divmod(hi, 16)
                    pa = j % 2
                    if h == 0:
                        p.wait(t, f"dSk{pa}", 16 * (j // 2 + 1))
                        if j == 0:
                            p.wait(t, "dHl", 16)
                            p.wait(t, "Vrun", NT)
                            p.wait(t, "Vc4", NT)
                            p.wait(t, "Set", NT)
                            p.wait(t, "Vw", NT + 1)
                    if hi >= 2:
                        p.wait(t, "Stb", 2 * (hi - 1))
                    off = pa * 3072 + 192 * h
                    t.transpose(PB[h % 2][0:128, 0:128], sPad[:, off:off + 128],
                                sIdB[:, :])
                    p.inc("Ttb", t.transpose(PB[h % 2][0:128, 128:256],
                                             sPad[:, off + 64:off + 192], sIdB[:, :]))
                g = hi - 2
                if g >= 0:
                    jg, hg = divmod(g, 16)
                    pag = jg % 2
                    if hi >= NT * 16:
                        p.wait(t, "Stb", 2 * (g + 1))
                    if hg == 0 and jg >= 1:
                        p.wait(t, f"dCl{(jg - 1) % 2}", 16 * ((jg - 1) // 2 + 1))
                    if g >= 4:
                        gb = g - 4
                        if gb % 2 == 0:
                            p.wait(t, "SbdV", gb // 2 + 1)
                        else:
                            p.wait(t, "SbdS", gb // 2 + 1)
                    to = 256 * (hg % 2)
                    if jg == 0:
                        cia = sCiH[0:64, 64 * hg:64 * (hg + 1)]
                    else:
                        cia = sCiLo[0:64, ((jg - 1) % 2) * D + 64 * hg:
                                     ((jg - 1) % 2) * D + 64 * (hg + 1)]
                    t.matmul(PF[2 + hg % 4][0:64, 0:128], cia,
                             sTab[0:64, to:to + 128], start=True, stop=False)
                    p.inc("Tbd", t.matmul(
                        PF[2 + hg % 4][0:64, 0:128],
                        sCi[0:128, jg * D + 64 * hg: jg * D + 64 * (hg + 1)],
                        sTab[0:128, to + 128:to + 256], start=False, stop=True))
                    if hg == 15:
                        p.wait(t, "SbdV", 8 * (jg + 1))
                        p.wait(t, "SbdS", 8 * (jg + 1))
                        if jg >= 1:
                            p.wait(t, "Vv", jg)
                        for hf in range(2):
                            last = None
                            for dc in range(8):
                                last = t.matmul(
                                    PF[hf][:, 0:512],
                                    sCLT[:, pag * D + 128 * dc: pag * D + 128 * (dc + 1)],
                                    sWfc[:, dc * 1024 + 512 * hf: dc * 1024 + 512 * (hf + 1)],
                                    start=(dc == 0), stop=(dc == 7))
                        p.inc("Tfc", last)

        @B.scalar
        def _(s):
            def ln_scalar(j):
                p.wait(s, "Vv", j + 1)
                if j >= 1:
                    p.wait(s, "Vsq2", j)
                p.inc("SC6", s.activation(sSq[:, :], sYv[:, :], AF.Square))
                p.wait(s, "Vmu1", j + 1)
                p.inc("SC6", s.activation(sSm[:, 4:5], sSm[:, 2:3], AF.Ln,
                                          bias=sEp6[:, 0:1]))
                p.wait(s, "SC6", p.n("SC6"))
                p.inc("SC6", s.activation(sSm[:, 6:7], sSm[:, 4:5], AF.Exp,
                                          scale=-0.5))
                p.wait(s, "Vy2", j + 1)
                p.wait(s, "Vsq1", j + 1)
                p.inc("SC6", s.activation(sSq[:, :], sYv[:, :], AF.Square))
                p.wait(s, "Vmu2", j + 1)
                p.inc("SC6", s.activation(sSm[:, 5:6], sSm[:, 3:4], AF.Ln,
                                          bias=sEp5[:, 0:1]))
                p.wait(s, "SC6", p.n("SC6"))
                p.inc("SC6", s.activation(sSm[:, 7:8], sSm[:, 5:6], AF.Exp,
                                          scale=-0.5))

            for hi in range(NT * 16 + 2):
                if hi < NT * 16:
                    j, h = divmod(hi, 16)
                    p.wait(s, "Ttb", hi + 1)
                    if hi >= 2:
                        p.wait(s, "Tbd", hi - 1)
                    to = 256 * (h % 2)
                    p.inc("Stb", s.copy(sTab[0:128, to:to + 256],
                                        PB[h % 2][0:128, 0:256]), 2)
                g = hi - 2
                if g >= 0 and g % 2 == 1:
                    jg, hg = divmod(g, 16)
                    p.wait(s, "Tbd", g + 1)
                    p.inc("SbdS", s.copy(
                        sCLT[64:128,
                             (jg % 2) * D + (hg // 2) * 128:
                             (jg % 2) * D + (hg // 2) * 128 + 128],
                        PF[2 + hg % 4][0:64, 0:128]))
                if g >= 0 and g % 16 == 15 and g // 16 >= 1:
                    ln_scalar(g // 16 - 1)
            ln_scalar(NT - 1)

        @B.vector
        def _(v):
            def ln_vec(j):
                pa = j % 2
                p.wait(v, "Tfc", j + 1)
                p.wait(v, f"dXr{pa}", 16 * (j // 2 + 1))
                v.tensor_add(sYv[:, 0:512], PF[0][:, 0:512],
                             sX2r[:, pa * D: pa * D + 512])
                p.inc("Vv", v.tensor_add(sYv[:, 512:1024], PF[1][:, 0:512],
                                         sX2r[:, pa * D + 512: pa * D + 1024]))
                p.wait(v, "Vv", j + 1)
                p.inc("VV", v.reduce_sum(sSm[:, 0:1], sYv[:, :], axis=AX.X))
                p.wait(v, "SC6", 6 * j + 1)
                p.inc("Vsq1", v.reduce_sum(sSm[:, 2:3], sSq[:, :], axis=AX.X))
                p.wait(v, "Vsq1", j + 1)
                p.inc("VV", v.tensor_scalar_mul(sSm[:, 0:1], sSm[:, 0:1], 1.0 / D))
                p.wait(v, "VV", p.n("VV"))
                p.inc("VV", v.tensor_mul(sSm[:, 4:5], sSm[:, 0:1], sSm[:, 0:1]))
                p.wait(v, "VV", p.n("VV"))
                p.inc("VV", v.tensor_scalar(sSm[:, 2:3], sSm[:, 2:3], 1.0 / D,
                                            None, ALU.mult))
                p.wait(v, "VV", p.n("VV"))
                p.inc("Vmu1", v.tensor_sub(sSm[:, 2:3], sSm[:, 2:3], sSm[:, 4:5]))
                p.wait(v, "SC6", 6 * j + 3)
                p.inc("VV", v.tensor_scalar(sXln[:, :], sYv[:, :], sSm[:, 0:1],
                                            sSm[:, 6:7], ALU.subtract, ALU.mult))
                p.wait(v, "VV", p.n("VV"))
                p.inc("VV", v.tensor_mul(sXln[:, :], sXln[:, :], sGmT[:, :]))
                p.wait(v, "VV", p.n("VV"))
                p.inc("VV", v.tensor_add(sXln[:, :], sXln[:, :], sBtT[:, :]))
                p.wait(v, "VV", p.n("VV"))
                p.inc("Vy2", v.tensor_add(sYv[:, :], sXln[:, :],
                                          sX2r[:, pa * D:(pa + 1) * D]))
                p.wait(v, "Vy2", j + 1)
                p.inc("VV", v.reduce_sum(sSm[:, 1:2], sYv[:, :], axis=AX.X))
                p.wait(v, "SC6", 6 * j + 4)
                p.inc("Vsq2", v.reduce_sum(sSm[:, 3:4], sSq[:, :], axis=AX.X))
                p.wait(v, "Vsq2", j + 1)
                p.inc("VV", v.tensor_scalar_mul(sSm[:, 1:2], sSm[:, 1:2], 1.0 / D))
                p.wait(v, "VV", p.n("VV"))
                p.inc("VV", v.tensor_mul(sSm[:, 5:6], sSm[:, 1:2], sSm[:, 1:2]))
                p.wait(v, "VV", p.n("VV"))
                p.inc("VV", v.tensor_scalar(sSm[:, 3:4], sSm[:, 3:4], 1.0 / D,
                                            None, ALU.mult))
                p.wait(v, "VV", p.n("VV"))
                p.inc("Vmu2", v.tensor_sub(sSm[:, 3:4], sSm[:, 3:4], sSm[:, 5:6]))
                p.wait(v, "SC6", 6 * j + 6)
                if j >= 2:
                    p.wait(v, f"dO{pa}", 16 * (j // 2))
                p.inc("Vx3", v.tensor_scalar(sX3[:, pa * D:(pa + 1) * D], sYv[:, :],
                                             sSm[:, 1:2], sSm[:, 7:8],
                                             ALU.subtract, ALU.mult))
                p.wait(v, "Vx3", j + 1)

            for j in range(NT):
                pa = j % 2
                for hg in range(0, 16, 2):
                    g = 16 * j + hg
                    p.wait(v, "Tbd", g + 1)
                    p.inc("SbdV", v.tensor_scalar(
                        sCLT[0:64,
                             pa * D + (hg // 2) * 128:
                             pa * D + (hg // 2) * 128 + 128],
                        PF[2 + hg % 4][0:64, 0:128], 1.0, None, ALU.mult))
                if j >= 1:
                    ln_vec(j - 1)
            ln_vec(NT - 1)

        B.flush()
    return nc


def stage_lc(inputs, x2_full, cs_all, cid_halos):
    W_pe = np.asarray(inputs["L1_attn_W_pe"], np.float32)
    Wd = W_pe[:, 0:31]
    Wg = W_pe[:, 31:62]
    Wdb = np.zeros((128, 62), np.float32)
    Wdb[0:64, 0:31] = Wd
    Wdb[64:128, 31:62] = Wd
    Wgb = np.zeros((128, 62), np.float32)
    Wgb[0:64, 0:31] = Wg
    Wgb[64:128, 31:62] = Wg
    b_pe = np.asarray(inputs["L1_attn_b_pe"], np.float32)
    bped = np.tile(b_pe[0:31], 16).reshape(1, 496)
    bpeg = np.tile(b_pe[31:62], 16).reshape(1, 496)
    sttr = np.asarray(inputs["L1_attn_stt_energy"], np.float32).reshape(1, 496)
    rsel = np.zeros((16, 496), np.float32)
    for h in range(16):
        rsel[h, 31 * h:31 * (h + 1)] = 1.0
    WfcT = _stage_w(np.asarray(inputs["L1_attn_W_fc"], np.float32).T)
    gmr = np.asarray(inputs["L1_attn_gamma"], np.float32).reshape(1, D)
    btr = np.asarray(inputs["L1_attn_beta"], np.float32).reshape(1, D)
    trif = np.triu(np.ones((128, 128), np.float32))
    nsl = -np.tril(np.ones((32, 32), np.float32), -1)
    identf = np.eye(128, dtype=np.float32)
    maps = []
    for cid in range(NCORE):
        t0 = RPC * cid
        rc = np.zeros((128, 16), np.float32)
        for jj in range(16):
            rc[:, jj] = 1.0 / (t0 + 128 * jj + np.arange(128) + 1.0)
        if cid == 0:
            rh = np.zeros((32, 1), np.float32)
            x2hv = np.zeros((32, D), np.float32)
            cm = np.ones((128, 496), np.float32)
            pk = np.arange(128)[:, None]
            kk = np.tile(np.arange(31)[None, :], (1, 16)).reshape(1, 496)
            cm = (pk + kk >= 30).astype(np.float32)
        else:
            rh = (1.0 / (t0 - 32 + np.arange(32) + 1.0)).astype(np.float32).reshape(32, 1)
            x2hv = np.ascontiguousarray(x2_full[t0 - 32:t0, :])
            cm = np.ones((128, 496), np.float32)
        pf = (np.arange(8) < cid).astype(np.float32).reshape(8, 1)
        maps.append({
            "x2i": np.ascontiguousarray(x2_full[t0:t0 + RPC, :]),
            "x2h": x2hv, "csall": cs_all, "prefm": pf,
            "Wdb": Wdb, "Wgb": Wgb, "bped": bped, "bpeg": bpeg, "sttr": sttr,
            "rsel": rsel, "WfcT": WfcT, "gmr": gmr, "btr": btr,
            "recipc": rc, "reciph": rh, "trif": trif, "nsl": nsl,
            "cmsk": cm, "identf": identf,
        })
    return maps



def build_ld():
    nc = bass.Bass("TRN2", target_bir_lowering=False)
    x3i = nc.dram_tensor("x3i", [RPC, D], F32, kind="ExternalInput").ap()
    ufT = nc.dram_tensor("ufT", [128, 8192], F32, kind="ExternalInput").ap()
    uiT = nc.dram_tensor("uiT", [128, 8192], F32, kind="ExternalInput").ap()
    ucT = nc.dram_tensor("ucT", [128, 8192], F32, kind="ExternalInput").ap()
    wfT = nc.dram_tensor("wfT", [128, 8192], F32, kind="ExternalInput").ap()
    wiT = nc.dram_tensor("wiT", [128, 8192], F32, kind="ExternalInput").ap()
    wcT = nc.dram_tensor("wcT", [128, 8192], F32, kind="ExternalInput").ap()
    woT = nc.dram_tensor("woT", [128, 8192], F32, kind="ExternalInput").ap()
    uoT = nc.dram_tensor("uoT", [128, 8192], F32, kind="ExternalInput").ap()
    incc = nc.dram_tensor("incc", [128, 8], F32, kind="ExternalInput").ap()
    incr = nc.dram_tensor("incr", [1, D], F32, kind="ExternalInput").ap()
    brows = nc.dram_tensor("brows", [1, 4 * D], F32, kind="ExternalInput").ap()
    identf = nc.dram_tensor("identf", [128, 128], F32, kind="ExternalInput").ap()
    x4s = nc.dram_tensor("x4s", [RPC, D], F32, kind="ExternalOutput").ap()

    I32 = mybir.dt.int32

    with ExitStack() as st:
        E = st.enter_context
        sX3 = E(nc.sbuf_tensor("sX3", [128, NT * D], F32))
        sWf = E(nc.sbuf_tensor("sWf", [128, 8192], BF16))
        sWi = E(nc.sbuf_tensor("sWi", [128, 8192], BF16))
        sWc = E(nc.sbuf_tensor("sWc", [128, 8192], BF16))
        sWo = E(nc.sbuf_tensor("sWo", [128, 8192], BF16))
        sWu = E(nc.sbuf_tensor("sWu", [128, 8192], BF16))
        sX3T = E(nc.sbuf_tensor("sX3T", [128, 2 * D], BF16))
        sC1T = E(nc.sbuf_tensor("sC1T", [128, 2 * D], BF16))
        sC1 = E(nc.sbuf_tensor("sC1", [128, D], F32))
        sIt = E(nc.sbuf_tensor("sIt", [128, D], F32))
        sGt = E(nc.sbuf_tensor("sGt", [128, D], F32))
        sHt = sGt
        sC0t = E(nc.sbuf_tensor("sC0t", [128, D], F32))
        sIcc = E(nc.sbuf_tensor("sIcc", [128, 8], BF16))
        sIcr = E(nc.sbuf_tensor("sIcr", [1, D], F32))
        sRfb = E(nc.sbuf_tensor("sRfb", [1, 4 * D], BF16))
        sOnRb = E(nc.sbuf_tensor("sOnRb", [1, 128], BF16))
        sSq = E(nc.sbuf_tensor("sSq", [128, D], F32))
        sX4 = E(nc.sbuf_tensor("sX4", [128, 2 * D], F32))
        sId = E(nc.sbuf_tensor("sId", [128, 128], F32))
        sOnR = E(nc.sbuf_tensor("sOnR", [1, 128], F32))
        sSm = E(nc.sbuf_tensor("sSm", [128, 8], F32))
        sMgk = E(nc.sbuf_tensor("sMgk", [128, 1], I32))
        PF = [E(nc.psum_tensor(f"PF{i}", [128, 512], F32)) for i in range(8)]

        p = Prog(nc, st)
        B = DeferredBlock(p.block)

        @B.sync
        def _(sy):
            p.inc("dA", sy.dma_start(out=sX3[:, :].rearrange("p (t d) -> p t d", d=D),
                  in_=x3i.rearrange("(t p) d -> p t d", p=128)), 16)
            p.inc("dA", sy.dma_start(out=sIcr[:, :], in_=incr[:, :]), 16)
            p.inc("dA", sy.dma_start(out=sId[:, :], in_=identf[:, :]), 16)

        @B.vector
        def _(v):
            p.inc("VV", v.memset(sOnR[:, :], 1.0))
            p.inc("VV", v.memset(sOnRb[:, :], 1.0))
            p.inc("VV", v.memset(sMgk[:, :], 0x5f3759df))

        @B.gpsimd
        def _(g):
            p.inc("dIc", g.dma_start(out=sIcc[:, :], in_=incc[:, :]), 16)
            p.inc("dBr", g.dma_start(out=sRfb[:, :], in_=brows[:, :]), 16)
            p.inc("dU0", g.dma_start(out=sWf[:, :], in_=ufT[:, :]), 16)
            p.wait(g, "Tmv", 2)
            p.inc("dU1", g.dma_start(out=sWf[:, :], in_=uiT[:, :]), 16)
            p.wait(g, "Tmv", 4)
            p.inc("dU2", g.dma_start(out=sWf[:, :], in_=ucT[:, :]), 16)
            p.wait(g, "Tmv", 6)
            p.inc("dW0", g.dma_start(out=sWf[:, :], in_=wfT[:, :]), 16)
            p.inc("dW1", g.dma_start(out=sWi[:, :], in_=wiT[:, :]), 16)
            p.inc("dW2", g.dma_start(out=sWc[:, :], in_=wcT[:, :]), 16)
            p.inc("dW3", g.dma_start(out=sWo[:, :], in_=woT[:, :]), 16)
            p.inc("dW4", g.dma_start(out=sWu[:, :], in_=uoT[:, :]), 16)

        @B.tensor
        def _(t):
            p.wait(t, "dA", 16 * 3)
            p.wait(t, "dIc", 16)
            p.wait(t, "dBr", 16)
            p.wait(t, "VV", 3)
            p.inc("Tc0", t.matmul(PF[2][:, 0:512], sOnR[0:1, :], sIcr[0:1, 0:512],
                                  start=True, stop=True))
            p.inc("Tc0", t.matmul(PF[3][:, 0:512], sOnR[0:1, :], sIcr[0:1, 512:1024],
                                  start=True, stop=True))
            for u in range(3):
                p.wait(t, f"dU{u}", 16)
                if u > 0:
                    p.wait(t, "Smv", 2 * u)
                for hf in range(2):
                    for kc in range(8):
                        t.matmul(PF[4 + hf][0:1, 0:512], sIcc[:, kc:kc + 1],
                                 sWf[:, kc * 1024 + 512 * hf: kc * 1024 + 512 * (hf + 1)],
                                 start=(kc == 0), stop=False)
                    p.inc("Tmv", t.matmul(
                        PF[4 + hf][0:1, 0:512], sOnRb[0:1, 0:1],
                        sRfb[0:1, u * D + 512 * hf: u * D + 512 * (hf + 1)],
                        start=False, stop=True))

        @B.scalar
        def _(s):
            p.wait(s, "Tc0", 2)
            p.inc("Sc0", s.copy(sC0t[:, 0:512], PF[2][:, 0:512]))
            p.inc("Sc0", s.copy(sC0t[:, 512:1024], PF[3][:, 0:512]))
            for u in range(3):
                for hf in range(2):
                    p.wait(s, "Tmv", 2 * u + hf + 1)
                    p.inc("Smv", s.copy(
                        sRfb[0:1, u * D + 512 * hf: u * D + 512 * (hf + 1)],
                        PF[4 + hf][0:1, 0:512]))

        trn = [0]

        def emit_x3t(t, j):
            for c in range(8):
                k = trn[0]
                if k >= 4:
                    p.wait(t, "Str", k - 3)
                p.inc("Ttr", t.transpose(PF[k % 4][:, 0:128],
                                         sX3[:, j * D + 128 * c: j * D + 128 * (c + 1)],
                                         sId[:, :]))
                trn[0] += 1

        def emit_c1t(t, j):
            for c in range(8):
                k = trn[0]
                if k >= 4:
                    p.wait(t, "Str", k - 3)
                p.inc("Ttr", t.transpose(PF[k % 4][:, 0:128],
                                         sC1[:, 128 * c:128 * (c + 1)], sId[:, :]))
                trn[0] += 1

        @B.tensor
        def _(t):
            p.wait(t, "Smv", 6)
            p.wait(t, "Sc0", 2)
            for nm in ("dW0", "dW1", "dW2", "dW3", "dW4"):
                p.wait(t, nm, 16)
            emit_x3t(t, 0)
            for j in range(NT):
                pa = j % 2
                p.wait(t, "Str", 16 * j + 8)
                if j >= 1:
                    p.wait(t, "Sg", 6 * j)
                for gi, wsb in enumerate((sWf, sWi, sWc)):
                    b0 = 4 if gi % 2 == 0 else 6
                    if gi == 1 and j >= 1:
                        p.wait(t, "Sh", 2 * j)
                    if gi == 2:
                        p.wait(t, "Sg", 6 * j + 2)
                    for hf in range(2):
                        for kc in range(8):
                            t.matmul(PF[b0 + hf][:, 0:512],
                                     sX3T[:, pa * D + 128 * kc: pa * D + 128 * (kc + 1)],
                                     wsb[:, kc * 1024 + 512 * hf: kc * 1024 + 512 * (hf + 1)],
                                     start=(kc == 0), stop=False)
                        p.inc("Tg", t.matmul(
                            PF[b0 + hf][:, 0:512], sOnRb[0:1, :],
                            sRfb[0:1, gi * D + 512 * hf: gi * D + 512 * (hf + 1)],
                            start=False, stop=True))
                if j + 1 < NT:
                    emit_x3t(t, j + 1)
                p.wait(t, "Vc1", j + 1)
                emit_c1t(t, j)
                p.wait(t, "Str", trn[0])
                p.wait(t, "Sg", 6 * j + 4)
                for hf in range(2):
                    for kc in range(8):
                        t.matmul(PF[6 + hf][:, 0:512],
                                 sX3T[:, pa * D + 128 * kc: pa * D + 128 * (kc + 1)],
                                 sWo[:, kc * 1024 + 512 * hf: kc * 1024 + 512 * (hf + 1)],
                                 start=(kc == 0), stop=False)
                    for kc in range(8):
                        t.matmul(PF[6 + hf][:, 0:512],
                                 sC1T[:, pa * D + 128 * kc: pa * D + 128 * (kc + 1)],
                                 sWu[:, kc * 1024 + 512 * hf: kc * 1024 + 512 * (hf + 1)],
                                 start=False, stop=False)
                    p.inc("Th", t.matmul(
                        PF[6 + hf][:, 0:512], sOnRb[0:1, :],
                        sRfb[0:1, 3 * D + 512 * hf: 3 * D + 512 * (hf + 1)],
                        start=False, stop=True))

        @B.scalar
        def _(s):
            etn = [0]

            def evac_tr(dst_base, pa_, j_, kind):
                for c in range(8):
                    k = etn[0]
                    p.wait(s, "Ttr", k + 1)
                    if kind == "x":
                        dst = sX3T[:, pa_ * D + 128 * c: pa_ * D + 128 * (c + 1)]
                    else:
                        dst = sC1T[:, pa_ * D + 128 * c: pa_ * D + 128 * (c + 1)]
                    p.inc("Str", s.copy(dst, PF[k % 4][:, 0:128]))
                    etn[0] += 1

            evac_tr(None, 0, 0, "x")
            for j in range(NT):
                pa = j % 2
                for gi, (buf, fn) in enumerate(((sC1, AF.Sigmoid), (sIt, AF.Sigmoid),
                                                (sGt, AF.Tanh))):
                    b0 = 4 if gi % 2 == 0 else 6
                    for hf in range(2):
                        p.wait(s, "Tg", 6 * j + 2 * gi + hf + 1)
                        if j >= 1 and gi == 0:
                            p.wait(s, "Vc1", j)
                        if j >= 1 and gi == 2 and hf == 0:
                            p.wait(s, "Vx4", j)
                        p.inc("Sg", s.activation(buf[:, 512 * hf:512 * (hf + 1)],
                                                 PF[b0 + hf][:, 0:512], fn))
                if j + 1 < NT:
                    if j >= 1:
                        p.wait(s, "Th", 2 * j)
                    evac_tr(None, (j + 1) % 2, j + 1, "x")
                evac_tr(None, pa, j, "c")
                for hf in range(2):
                    p.wait(s, "Th", 2 * j + hf + 1)
                    if j >= 1 and hf == 0:
                        p.wait(s, "Vx4", j)
                    p.inc("Sh", s.activation(sHt[:, 512 * hf:512 * (hf + 1)],
                                             PF[6 + hf][:, 0:512], AF.Tanh))
                p.wait(s, "Vy4", j + 1)
                if j >= 1:
                    p.wait(s, "Vsq4", j)
                p.inc("Ssq4", s.activation(sSq[:, :], sHt[:, :], AF.Square))

        @B.vector
        def _(v):
            for j in range(NT):
                pa = j % 2
                p.wait(v, "Sg", 6 * (j + 1))
                p.inc("VV", v.tensor_mul(sC1[:, :], sC1[:, :], sC0t[:, :]))
                p.inc("VV", v.tensor_mul(sIt[:, :], sIt[:, :], sGt[:, :]))
                p.wait(v, "VV", p.n("VV"))
                p.inc("Vc1", v.tensor_add(sC1[:, :], sC1[:, :], sIt[:, :]))
                p.wait(v, "Sh", 2 * (j + 1))
                p.inc("Vy4", v.tensor_add(sHt[:, :], sHt[:, :],
                                          sX3[:, j * D:(j + 1) * D]))
                p.wait(v, "Vy4", j + 1)
                p.inc("VV", v.reduce_sum(sSm[:, 0:1], sHt[:, :], axis=AX.X))
                p.wait(v, "Ssq4", j + 1)
                p.inc("Vsq4", v.reduce_sum(sSm[:, 2:3], sSq[:, :], axis=AX.X))
                p.wait(v, "Vsq4", j + 1)
                p.inc("VV", v.tensor_scalar_mul(sSm[:, 0:1], sSm[:, 0:1], 1.0 / D))
                p.wait(v, "VV", p.n("VV"))
                p.inc("VV", v.tensor_mul(sSm[:, 4:5], sSm[:, 0:1], sSm[:, 0:1]))
                p.wait(v, "VV", p.n("VV"))
                p.inc("VV", v.tensor_scalar(sSm[:, 2:3], sSm[:, 2:3], 1.0 / D, None,
                                            ALU.mult))
                p.wait(v, "VV", p.n("VV"))
                p.inc("VV", v.tensor_sub(sSm[:, 2:3], sSm[:, 2:3], sSm[:, 4:5]))
                p.wait(v, "VV", p.n("VV"))
                p.inc("VV", v.tensor_scalar(sSm[:, 2:3], sSm[:, 2:3], 1e-5, None,
                                            ALU.add))
                p.wait(v, "VV", p.n("VV"))
                vcol = sSm[:, 2:3]
                r = sSm[:, 6:7]
                t1 = sSm[:, 5:6]
                p.inc("VV", v.tensor_scalar(r.bitcast(I32), vcol.bitcast(I32), 1, None,
                                            ALU.arith_shift_right))
                p.wait(v, "VV", p.n("VV"))
                p.inc("VV", v.tensor_sub(r.bitcast(I32), sMgk[:, 0:1], r.bitcast(I32)))
                for _i in range(3):
                    p.wait(v, "VV", p.n("VV"))
                    p.inc("VV", v.tensor_mul(t1, r, r))
                    p.wait(v, "VV", p.n("VV"))
                    p.inc("VV", v.tensor_mul(t1, t1, vcol))
                    p.wait(v, "VV", p.n("VV"))
                    p.inc("VV", v.tensor_scalar(t1, t1, -0.5, 1.5, ALU.mult, ALU.add))
                    p.wait(v, "VV", p.n("VV"))
                    p.inc("VV", v.tensor_mul(r, r, t1))
                p.wait(v, "VV", p.n("VV"))
                if j >= 2:
                    p.wait(v, f"dO{pa}", 16 * (j // 2))
                p.inc("Vx4", v.tensor_scalar(sX4[:, pa * D:(pa + 1) * D], sHt[:, :],
                                             sSm[:, 0:1], sSm[:, 6:7],
                                             ALU.subtract, ALU.mult))

        @B.sync
        def _(sy):
            for j in range(NT):
                p.wait(sy, "Vx4", j + 1)
                p.inc(f"dO{j % 2}", sy.dma_start(out=x4s[128 * j:128 * (j + 1), :],
                      in_=sX4[:, (j % 2) * D:(j % 2 + 1) * D]), 16)

        @B.sync
        def _(sy):
            p.wait(sy, "dO0", 8 * 16)
            p.wait(sy, "dO1", 8 * 16)

        B.flush()
    return nc


def stage_ld(inputs, x3_full):
    f32 = lambda k: np.asarray(inputs[k], np.float32)
    stag = lambda k: _stage_w(f32(k).T)
    ufT, uiT, ucT = stag("L1_ffn_uf"), stag("L1_ffn_ui"), stag("L1_ffn_uc")
    wfT, wiT, wcT = stag("L1_ffn_wf"), stag("L1_ffn_wi"), stag("L1_ffn_wc")
    woT, uoT = stag("L1_ffn_wo"), stag("L1_ffn_uo")
    in_c = f32("L1_ffn_in_c")
    incc = np.ascontiguousarray(in_c.reshape(8, 128).T)
    incr = in_c.reshape(1, D)
    brows = np.concatenate([f32("L1_ffn_bf"), f32("L1_ffn_bi"),
                            f32("L1_ffn_bc"), f32("L1_ffn_bo")]).reshape(1, 4 * D)
    identf = np.eye(128, dtype=np.float32)
    maps = []
    for cid in range(NCORE):
        maps.append({
            "x3i": np.ascontiguousarray(x3_full[RPC * cid:RPC * (cid + 1), :]),
            "ufT": ufT, "uiT": uiT, "ucT": ucT, "wfT": wfT, "wiT": wiT,
            "wcT": wcT, "woT": woT, "uoT": uoT,
            "incc": incc, "incr": incr, "brows": brows, "identf": identf,
        })
    return maps


def stage_la(inputs, x_full):
    QS = 80
    Wq = _stage_w(inputs["L0_attn_Wq"])
    Wk = _stage_w(inputs["L0_attn_Wk"])
    Wv = _stage_w(inputs["L0_attn_Wv"])
    Wo = _stage_w(inputs["L0_attn_Wo"])
    xctx = np.ascontiguousarray(np.concatenate([x_full[0:4, :], x_full[T - 512:T, :]], axis=0))
    r = np.arange(NQ)[:, None]
    c = np.arange(NC0)[None, :]
    maskadd = np.where(c <= r - 124, 0.0, NEG).astype(np.float32)
    valid = (np.arange(NQ) >= 124).astype(np.float32)
    identf = np.eye(128, dtype=np.float32)
    maps = []
    for cid in range(NCORE):
        vld = np.zeros((128, 1), np.float32)
        vld[0:QS, 0] = valid[QS * cid:QS * (cid + 1)]
        tmask = np.ones((128, 16), np.float32)
        if cid == NCORE - 1:
            tmask[:, 11:16] = 0.0
        maps.append({
            "x0s": np.ascontiguousarray(x_full[RPC * cid:RPC * (cid + 1), :]),
            "xq": np.ascontiguousarray(x_full[Q0 + QS * cid:Q0 + QS * (cid + 1), :]),
            "xctx": xctx,
            "Wq": Wq, "Wk": Wk, "Wv": Wv, "Wo": Wo,
            "maskadd": np.ascontiguousarray(maskadd[QS * cid:QS * (cid + 1), :]),
            "validc": vld, "tmaskf": tmask,
            "identf": identf,
        })
    return maps


def kernel(**inputs):
    x_full = np.asarray(inputs["L0_attn_x"], dtype=np.float32).reshape(T, D)
    cores = list(range(NCORE))

    res_a = run_bass_kernel_spmd(build_la(), stage_la(inputs, x_full), core_ids=cores)
    x1 = np.concatenate([res_a.results[c]["x1s"] for c in cores], axis=0)
    x1[Q0:T] = np.concatenate([res_a.results[c]["x1q"] for c in cores], axis=0)
    mxs = np.concatenate([res_a.results[c]["mx"] for c in cores], axis=1)

    maps_b = stage_lb(inputs, x1)
    for m in maps_b:
        m["mxs"] = np.ascontiguousarray(mxs)
    res_b = run_bass_kernel_spmd(build_lb(), maps_b, core_ids=cores)
    x2 = np.concatenate([res_b.results[c]["x2s"] for c in cores], axis=0)
    cs_all = np.concatenate([res_b.results[c]["cs"] for c in cores], axis=0)

    res_c = run_bass_kernel_spmd(build_lc(),
                                 stage_lc(inputs, x2, np.ascontiguousarray(cs_all), None),
                                 core_ids=cores)
    x3 = np.concatenate([res_c.results[c]["x3s"] for c in cores], axis=0)

    res_d = run_bass_kernel_spmd(build_ld(), stage_ld(inputs, x3), core_ids=cores)
    x4 = np.concatenate([res_d.results[c]["x4s"] for c in cores], axis=0)
    return x4.reshape(1, T, D)
